# version 14
# baseline (speedup 1.0000x reference)
import math
import numpy as np
import ml_dtypes
from contextlib import ExitStack

import concourse.bass as bass
import concourse.mybir as mybir
from concourse.bass_utils import run_bass_kernel_spmd

F32 = mybir.dt.float32
BF16 = mybir.dt.bfloat16
AF = mybir.ActivationFunctionType
ALU = mybir.AluOpType
AX = mybir.AxisListType

L, D, H, HD, FF, R = 4096, 1024, 16, 64, 4096, 16
NT = L // 128
QB, NQB, NP = 512, 8, 8
EPS = 1e-5
SHIFT = 43.0
VW = D

ENGS = ("tensor", "vector", "scalar", "gpsimd", "sync")


class Em:

    def __init__(self, nc):
        self.nc = nc
        self.sems = {}
        self.cnt = {}
        self.q = None
        self.wm = None

    def add_sem(self, name, handle):
        self.sems[name] = handle
        self.cnt[name] = 0

    def phase(self):
        self.q = {e: [] for e in ENGS}
        self.wm = {}
        self.meta = {e: [] for e in ENGS}

    def wait(self, eng, sem, val):
        if val is None or val <= 0:
            return
        key = (eng, sem)
        if self.wm.get(key, -1) >= val:
            return
        self.wm[key] = val
        h = self.sems[sem]
        self.meta[eng].append(("wait", sem, val))
        self.q[eng].append(lambda E: E.wait_ge(h, val))

    def op(self, eng, fn, sig=None, n=1):
        if sig is None:
            self.meta[eng].append(("op", None, 0))
            self.q[eng].append(lambda E: fn(E))
            return None
        self.cnt[sig] += n
        h = self.sems[sig]
        self.meta[eng].append(("op", sig, n))
        self.q[eng].append(lambda E: fn(E).then_inc(h, n))
        return self.cnt[sig]

    def check_deadlock(self, start_counts):
        cnt = dict(start_counts)
        pos = {e: 0 for e in ENGS}
        progress = True
        while progress:
            progress = False
            for e in ENGS:
                while pos[e] < len(self.meta[e]):
                    kind, sem, val = self.meta[e][pos[e]]
                    if kind == "wait" and cnt.get(sem, 0) < val:
                        break
                    if kind == "op" and sem is not None:
                        cnt[sem] = cnt.get(sem, 0) + val
                    pos[e] += 1
                    progress = True
        stuck = {e: (pos[e], len(self.meta[e]), self.meta[e][pos[e]]
                     if pos[e] < len(self.meta[e]) else None)
                 for e in ENGS if pos[e] < len(self.meta[e])}
        if stuck:
            raise AssertionError(f"emission deadlock: {stuck}; counts={ {k: v for k, v in cnt.items()} }")

    def dma(self, eng, sig, out, in_, **kw):
        return self.op(eng, lambda E: E.dma_start(out, in_, **kw), sig, 16)

    def tread(self, eng, sig, out, in_):
        return self.op(eng, lambda E: E.dma_start(out, in_, transpose=True), sig, 16)

    def barrier_counts(self):
        return dict(self.cnt)

    def wait_all(self, counts):
        for e in ENGS:
            for s, v in counts.items():
                self.wait(e, s, v)

    def replay(self, block):
        qs = self.q

        @block.tensor
        def _(E):
            for f in qs["tensor"]:
                f(E)

        @block.vector
        def _(E):
            for f in qs["vector"]:
                f(E)

        @block.scalar
        def _(E):
            for f in qs["scalar"]:
                f(E)

        @block.gpsimd
        def _(E):
            for f in qs["gpsimd"]:
                f(E)

        @block.sync
        def _(E):
            for f in qs["sync"]:
                f(E)


def emit_ln(em, x_ap, n_free, scr, out_ap):
    fs = scr["fs"]

    def sfence():
        return em.op("scalar", lambda E: E.activation(
            fs[0:1, 2:3], fs[0:1, 3:4], AF.Square, bias=0.0, scale=1.0), "SS")

    v1 = em.op("vector", lambda E: E.tensor_reduce(
        scr["sumx"][:, :], x_ap, AX.X, ALU.add), "VS")
    ss_sq = em.op("scalar", lambda E: E.activation(
        scr["sq"][:, 0:n_free], x_ap, AF.Square, bias=0.0, scale=1.0,
        accum_out=scr["ssq"][:, :]), "SS")
    em.wait("scalar", "SS", ss_sq)
    ssf = sfence()
    em.wait("vector", "SS", ssf)
    em.wait("vector", "VS", v1)
    v2 = em.op("vector", lambda E: E.tensor_scalar(
        scr["mean"][:, :], scr["sumx"][:, :], 1.0 / n_free, None, ALU.mult), "VS")
    em.wait("vector", "VS", v2)
    v3 = em.op("vector", lambda E: E.tensor_tensor(
        scr["m2"][:, :], scr["mean"][:, :], scr["mean"][:, :], ALU.mult), "VS")
    em.wait("vector", "VS", v3)
    v4 = em.op("vector", lambda E: E.scalar_tensor_tensor(
        scr["var"][:, :], scr["ssq"][:, :], 1.0 / n_free, scr["m2"][:, :],
        ALU.mult, ALU.subtract), "VS")
    em.wait("vector", "VS", v4)
    v5 = em.op("vector", lambda E: E.tensor_scalar(
        scr["var"][:, :], scr["var"][:, :], EPS, None, ALU.add), "VS")
    em.wait("vector", "VS", v5)
    vs_r = em.op("vector", lambda E: E.reciprocal(
        scr["rcp"][:, :], scr["var"][:, :]), "VS")
    em.wait("vector", "VS", vs_r)
    vs_rf = em.op("vector", lambda E: E.tensor_copy(
        fs[0:1, 0:1], fs[0:1, 1:2]), "VS")
    em.wait("scalar", "VS", vs_rf)
    s1 = em.op("scalar", lambda E: E.activation(
        scr["lnt"][:, :], scr["rcp"][:, :], AF.Ln, bias=0.0, scale=1.0), "SS")
    em.wait("scalar", "SS", s1)
    ss_rs = em.op("scalar", lambda E: E.activation(
        scr["rs"][:, :], scr["lnt"][:, :], AF.Exp, bias=0.0, scale=0.5), "SS")
    em.wait("scalar", "SS", ss_rs)
    ssf = sfence()
    em.wait("vector", "SS", ssf)
    v6 = em.op("vector", lambda E: E.scalar_tensor_tensor(
        scr["nmrs"][:, :], scr["mean"][:, :], -1.0, scr["rs"][:, :],
        ALU.mult, ALU.mult), "VS")
    em.wait("vector", "VS", v6)
    vs_ap = em.op("vector", lambda E: E.tensor_scalar(
        out_ap, x_ap, scr["rs"][:, :], scr["nmrs"][:, :], ALU.mult, ALU.add), "VS")
    return vs_ap, ss_sq


def ln_stat(em, x_ap, n_free, scr):
    em.op("vector", lambda E: E.tensor_reduce(scr["sumx"][:, :], x_ap, AX.X, ALU.add))
    em.op("scalar", lambda E: E.activation(
        scr["sq"][:, 0:n_free], x_ap, AF.Square, bias=0.0, scale=1.0,
        accum_out=scr["ssq"][:, :]), "SS")
    fs = scr["fs"]
    return em.op("scalar", lambda E: E.activation(
        fs[0:1, 2:3], fs[0:1, 3:4], AF.Square, bias=0.0, scale=1.0), "SS")


def ln_fin(em, n_free, scr, ss_tok):
    fs = scr["fs"]
    em.wait("vector", "SS", ss_tok)
    v2 = em.op("vector", lambda E: E.tensor_scalar(
        scr["mean"][:, :], scr["sumx"][:, :], 1.0 / n_free, None, ALU.mult), "VS")
    em.wait("vector", "VS", v2)
    v3 = em.op("vector", lambda E: E.tensor_tensor(
        scr["m2"][:, :], scr["mean"][:, :], scr["mean"][:, :], ALU.mult), "VS")
    em.wait("vector", "VS", v3)
    v4 = em.op("vector", lambda E: E.scalar_tensor_tensor(
        scr["var"][:, :], scr["ssq"][:, :], 1.0 / n_free, scr["m2"][:, :],
        ALU.mult, ALU.subtract), "VS")
    em.wait("vector", "VS", v4)
    v5 = em.op("vector", lambda E: E.tensor_scalar(
        scr["var"][:, :], scr["var"][:, :], EPS, None, ALU.add), "VS")
    em.wait("vector", "VS", v5)
    vs_r = em.op("vector", lambda E: E.reciprocal(
        scr["rcp"][:, :], scr["var"][:, :]), "VS")
    em.wait("vector", "VS", vs_r)
    vs_rf = em.op("vector", lambda E: E.tensor_copy(
        fs[0:1, 0:1], fs[0:1, 1:2]), "VS")
    em.wait("scalar", "VS", vs_rf)
    s1 = em.op("scalar", lambda E: E.activation(
        scr["lnt"][:, :], scr["rcp"][:, :], AF.Ln, bias=0.0, scale=1.0), "SS")
    em.wait("scalar", "SS", s1)
    em.op("scalar", lambda E: E.activation(
        scr["rs"][:, :], scr["lnt"][:, :], AF.Exp, bias=0.0, scale=0.5), "SS")
    return em.op("scalar", lambda E: E.activation(
        fs[0:1, 2:3], fs[0:1, 3:4], AF.Square, bias=0.0, scale=1.0), "SS")


def ln_apply(em, x_ap, scr, ss_tok, out_ap):
    em.wait("vector", "SS", ss_tok)
    v6 = em.op("vector", lambda E: E.scalar_tensor_tensor(
        scr["nmrs"][:, :], scr["mean"][:, :], -1.0, scr["rs"][:, :],
        ALU.mult, ALU.mult), "VS")
    em.wait("vector", "VS", v6)
    return em.op("vector", lambda E: E.tensor_scalar(
        out_ap, x_ap, scr["rs"][:, :], scr["nmrs"][:, :], ALU.mult, ALU.add), "VS")


def ln_scratch(es, nc, tag, n_free):
    scr = {"sq": es.enter_context(nc.sbuf_tensor(f"sq_{tag}", [128, n_free], BF16))}
    for k in ["ssq", "mean", "m2", "var", "rcp", "lnt", "rs", "nmrs", "sumx"]:
        scr[k] = es.enter_context(nc.sbuf_tensor(f"{tag}_{k}", [128, 1], F32))
    scr["fs"] = es.enter_context(nc.sbuf_tensor(f"{tag}_fs", [1, 4], F32))
    return scr


def emit_A(em, g):
    (wq_sb, embq_sb, embk_sb, bqkv_sb, xt, x1b, x1t, qkvb, rtmp, qo, ko, vo,
     lnA, hn, qn, kn) = (g[k] for k in [
        "wq_sb", "embq_sb", "embk_sb", "bqkv_sb", "xt", "x1b", "x1t", "qkvb",
        "rtmp", "qo", "ko", "vo", "lnA", "hn", "qn", "kn"])
    (nbtb, qm2, km2, Bqk, Bqkb, Btq, pBn, nb164) = (g[k] for k in [
        "nbtb", "qm2", "km2", "Bqk", "Bqkb", "Btq", "pBn", "nb164"])
    (wp_in, wp_t, wp_g, wp_o, wp_s) = (g[k] for k in [
        "wp_in", "wp_t", "wp_g", "wp_o", "wp_s"])
    (uT_sb, fb_sb, gate_sb, fbg, fbgb) = (g[k] for k in [
        "uT_sb", "fb_sb", "gate_sb", "fbg", "fbgb"])
    qkv_ps, tp_ps, ms_ps = g["qkv_ps"], g["tp_ps"], g["ms_ps"]
    (ident, ones32, negB, bout_sb, gW2, vT) = (g[k] for k in [
        "ident", "ones32", "negB", "bout_sb", "gW2", "vT"])
    (x_in, wqkv, bqkv_r, bout_r, wfc_in, wpj_in, uT_in, vT_in,
     fb_in, gate_r, embq_in, embk_in, ident_in) = (g[k] for k in [
        "x_in", "wqkv", "bqkv_r", "bout_r", "wfc_in", "wpj_in",
        "uT_in", "vT_in", "fb_in", "gate_r", "embq_in", "embk_in", "ident_in"])
    (s_q, s_k, s_v, s_fceff, s_fcT, s_pjeff, s_nb, s_x1) = (g[k] for k in [
        "s_q", "s_k", "s_v", "s_fceff", "s_fcT", "s_pjeff", "s_nb", "s_x1"])
    sq2 = lnA["sq"]

    wq3 = wqkv.rearrange("(c p) n -> p c n", p=128)
    wq_dst = wq_sb[:, :].rearrange("p (c n) -> p c n", c=8)
    gq_wq = 0
    for h in range(2):
        gq_wq = em.dma("gpsimd", "GQ", wq_dst[:, 4 * h:4 * h + 4, :],
                       wq3[:, 4 * h:4 * h + 4, :])
    gq_id = em.dma("gpsimd", "GQ", ident[:, :], ident_in)
    gq_uT = em.dma("gpsimd", "GQ", uT_sb[0:R, :], uT_in)
    gq_vT = em.dma("gpsimd", "GQ", vT[:, :], vT_in)
    em.op("gpsimd", lambda E: E.memset(ones32[:, :], 1.0))
    em.op("gpsimd", lambda E: E.memset(qn[:, :], 0.0))
    gs_init = em.op("gpsimd", lambda E: E.memset(kn[:, :], 0.0), "GS")

    lq_embq = em.dma("sync", "LQ", embq_sb[:, :], embq_in)
    lq_embk = em.dma("sync", "LQ", embk_sb[:, :], embk_in)
    lq_bq = em.dma("sync", "LQ", bqkv_sb[:, :], bqkv_r)
    em.dma("sync", "LQ", bout_sb[:, :], bout_r)
    em.dma("sync", "LQ", fb_sb[0:R, :], fb_in)
    lq_gate = em.dma("sync", "LQ", gate_sb[0:R, :], gate_r)
    x_tok = x_in.rearrange("(t p) d -> t p d", p=128)
    lq_x = {i: em.dma("sync", "LQ", xt[i][:, :], x_tok[i]) for i in range(2)}

    wfc_t = wfc_in.rearrange("(t p) d -> t p d", p=128)
    NU = 48

    def prep_src(j):
        if j < 32:
            return wfc_t[j]
        tile, s = (j - 32) // 2, (j - 32) % 2
        return wpj_in[128 * tile:128 * (tile + 1), 2048 * s:2048 * (s + 1)]

    def prep_w(j):
        return D if j < 32 else 2048

    pq_wp = {0: em.dma("scalar", "PQ", wp_in[0][:, 0:D], prep_src(0)),
             1: em.dma("scalar", "PQ", wp_in[1][:, 0:D], prep_src(1))}

    prep_state = {"done": 0, "fc_store": 0, "pj_store": 0}

    def prep_unit(j):
        sl = j % 2
        w = prep_w(j)
        src = wp_in[sl]
        em.wait("scalar", "PQ", pq_wp[j])
        is_pj = j >= 32
        half2 = is_pj and ((j - 32) % 2 == 1)
        scol = 1 if half2 else 0
        ss_a = em.op("scalar", lambda E, w=w, scol=scol, sl=sl: E.activation(
            wp_t[:, 0:w], wp_in[sl][:, 0:w], AF.Abs, bias=0.0, scale=1.0,
            accum_out=wp_s[:, scol:scol + 1]), "SS")
        em.wait("scalar", "SS", ss_a)
        ss_a = em.op("scalar", lambda E: E.activation(
            g["fscB"][0:1, 2:3], g["fscB"][0:1, 3:4], AF.Square,
            bias=0.0, scale=1.0), "SS")
        def emit_scale_and_tern(jj, sl2, w2):
            denom = D if jj < 32 else FF
            if jj < 32:
                vs_sc = em.op("vector", lambda E: E.tensor_scalar(
                    wp_s[:, 2:3], wp_s[:, 0:1], 1.0 / denom, 1e-5, ALU.mult, ALU.max), "VS")
            else:
                v_add = em.op("vector", lambda E: E.tensor_tensor(
                    wp_s[:, 2:3], wp_s[:, 0:1], wp_s[:, 1:2], ALU.add), "VS")
                em.wait("vector", "VS", v_add)
                vs_sc = em.op("vector", lambda E: E.tensor_scalar(
                    wp_s[:, 2:3], wp_s[:, 2:3], 1.0 / denom, 1e-5, ALU.mult, ALU.max), "VS")
            em.wait("vector", "VS", vs_sc)
            vs_rp = em.op("vector", lambda E: E.reciprocal(wp_s[:, 3:4], wp_s[:, 2:3]), "VS")
            em.wait("vector", "VS", vs_rp)
            v_ng = em.op("vector", lambda E: E.tensor_scalar(
                wp_s[:, 2:3], wp_s[:, 2:3], -1.0, None, ALU.mult), "VS")
            em.wait("vector", "VS", v_ng)
            vs_t = em.op("vector", lambda E, sl2=sl2, w2=w2: E.tensor_scalar(
                wp_t[:, 0:w2], wp_in[sl2][:, 0:w2], wp_s[:, 3:4], None, ALU.mult), "VS")
            em.op("vector", lambda E, w2=w2: E.tensor_scalar(
                wp_g[:, 0:w2], wp_t[:, 0:w2], 0.5, None, ALU.is_ge))
            em.op("vector", lambda E, w2=w2: E.scalar_tensor_tensor(
                wp_g[:, 0:w2], wp_t[:, 0:w2], -0.5, wp_g[:, 0:w2],
                ALU.is_le, ALU.subtract))
            vs_o = em.op("vector", lambda E, sl2=sl2, w2=w2: E.tensor_scalar(
                wp_o[sl2][:, 0:w2], wp_g[:, 0:w2], wp_s[:, 2:3], None, ALU.mult), "VS")
            em.wait("scalar", "VS", vs_o)
            return vs_t

        if not is_pj:
            em.wait("vector", "SS", ss_a)
            vs_t = emit_scale_and_tern(j, sl, D)
            dst = s_fceff.rearrange("(t p) d -> t p d", p=128)[j]
            prep_state["fc_store"] = em.dma("scalar", "PQ", dst, wp_o[sl][:, 0:D])
            jn = j + 2
            if jn < NU:
                em.wait("scalar", "VS", vs_t)
                pq_wp[jn] = em.dma("scalar", "PQ", wp_in[sl][:, 0:prep_w(jn)],
                                   prep_src(jn))
        elif half2:
            em.wait("vector", "SS", ss_a)
            tile = (j - 32) // 2
            for hh in range(2):
                sl2 = (j - 1 + hh) % 2
                vs_t = emit_scale_and_tern(j, sl2, 2048)
                prep_state["pj_store"] = em.dma(
                    "scalar", "PQ",
                    s_pjeff[128 * tile:128 * (tile + 1), 2048 * hh:2048 * (hh + 1)],
                    wp_o[sl2][:, 0:2048])
                jn = j - 1 + hh + 2
                if jn < NU:
                    em.wait("scalar", "VS", vs_t)
                    pq_wp[jn] = em.dma("scalar", "PQ",
                                       wp_in[sl2][:, 0:prep_w(jn)], prep_src(jn))

    q3b = qkvb[:, 0:D].rearrange("p (h d) -> p h d", h=16)
    k3b = qkvb[:, D:2 * D].rearrange("p (h d) -> p h d", h=16)
    r3 = rtmp[:, :].rearrange("p (h d) -> p h d", h=16)

    last_evac = {ch: 0 for ch in range(6)}
    last_x1t_use = {0: 0, 1: 0}
    vs_norm_last = 0
    st_tok = {"q": {}, "k": {}, "v": {}}
    x1_sp = {}
    x1t_ld = {}

    def emit_lnx(i):
        sx = i % 2
        sl = i % 2
        em.wait("vector", "LQ", lq_x[i])
        em.wait("scalar", "LQ", lq_x[i])
        em.wait("scalar", "VS", vs_norm_last)
        em.wait("vector", "PQ", x1_sp.get(i - 2, 0))
        vs_x1, ss_x1 = emit_ln(em, xt[sx][:, :], D, lnA, x1b[sl][:, :])
        if i + 2 < NT:
            em.wait("sync", "VS", vs_x1)
            em.wait("sync", "SS", ss_x1)
            lq_x[i + 2] = em.dma("sync", "LQ", xt[(i + 2) % 2][:, :], x_tok[i + 2])
        em.wait("scalar", "VS", vs_x1)
        x1_sp[i] = em.dma("scalar", "PQ",
                          s_x1.rearrange("(t p) d -> t p d", p=128)[i], x1b[sl][:, :])

    def emit_x1t(gr):
        em.wait("sync", "PQ", x1_sp[4 * gr + 3])
        em.wait("sync", "TS", last_x1t_use[gr % 2])
        lq = 0
        for c in range(8):
            lq = em.tread("sync", "LQ", x1t[gr % 2][:, 512 * c:512 * (c + 1)],
                          s_x1[512 * gr:512 * (gr + 1), 128 * c:128 * (c + 1)])
        x1t_ld[gr] = lq

    def emit_qkv(i):
        gr = i // 4
        m = i % 4
        sl = i % 2
        em.wait("tensor", "LQ", x1t_ld[gr])
        em.wait("tensor", "GQ", gq_wq)
        ts_chunk = {}
        for ch in range(6):
            em.wait("tensor", "VS", last_evac[ch])
            for c in range(8):
                tok = em.op("tensor", lambda E, ch=ch, c=c, gr=gr, m=m: E.matmul(
                    qkv_ps[:, 512 * ch:512 * (ch + 1)],
                    x1t[gr % 2][:, 512 * c + 128 * m: 512 * c + 128 * (m + 1)],
                    wq_sb[:, 3072 * c + 512 * ch: 3072 * c + 512 * (ch + 1)],
                    start=(c == 0), stop=(c == 7)), "TS" if c == 7 else None)
            ts_chunk[ch] = tok
        last_x1t_use[gr % 2] = em.cnt["TS"]
        em.wait("vector", "LQ", lq_bq)
        for ch in range(6):
            em.wait("vector", "TS", ts_chunk[ch])
            last_evac[ch] = em.op("vector", lambda E, ch=ch: E.tensor_tensor(
                qkvb[:, 512 * ch:512 * (ch + 1)], qkv_ps[:, 512 * ch:512 * (ch + 1)],
                bqkv_sb[:, 512 * ch:512 * (ch + 1)], ALU.add), "VS")
        if i >= 2:
            em.wait("vector", "LQ", st_tok["v"][i - 2])
        em.wait("vector", "LQ", lq_embq)
        em.wait("vector", "LQ", lq_embk)
        em.wait("vector", "VS", last_evac[1])
        eq3 = embq_sb[:, HD * i:HD * (i + 1)].rearrange("p (o d) -> p o d", o=1) \
            .broadcast_to([128, 16, HD])
        em.op("vector", lambda E: E.tensor_tensor(
            r3[:, :, 0:32], q3b[:, :, 0:32], q3b[:, :, 1:64:2], ALU.subtract))
        v_r = em.op("vector", lambda E: E.tensor_tensor(
            r3[:, :, 32:64], q3b[:, :, 32:64], q3b[:, :, 0:64:2], ALU.add), "VS")
        em.wait("vector", "VS", v_r)
        vs_q = em.op("vector", lambda E, sl=sl, eq3=eq3: E.tensor_tensor(
            qo[sl][:, :].rearrange("p (h d) -> p h d", h=16), r3, eq3, ALU.mult), "VS")
        em.wait("vector", "VS", vs_q)
        em.wait("vector", "VS", last_evac[3])
        ek3 = embk_sb[:, HD * i:HD * (i + 1)].rearrange("p (o d) -> p o d", o=1) \
            .broadcast_to([128, 16, HD])
        em.op("vector", lambda E: E.tensor_tensor(
            r3[:, :, 0:32], k3b[:, :, 0:32], k3b[:, :, 1:64:2], ALU.subtract))
        v_r = em.op("vector", lambda E: E.tensor_tensor(
            r3[:, :, 32:64], k3b[:, :, 32:64], k3b[:, :, 0:64:2], ALU.add), "VS")
        em.wait("vector", "VS", v_r)
        vs_k = em.op("vector", lambda E, sl=sl, ek3=ek3: E.tensor_tensor(
            ko[sl][:, :].rearrange("p (h d) -> p h d", h=16), r3, ek3, ALU.mult), "VS")
        em.wait("vector", "VS", last_evac[5])
        vs_v = em.op("vector", lambda E, sl=sl: E.tensor_copy(
            vo[sl][:, :], qkvb[:, 2 * D:3 * D]), "VS")
        em.wait("scalar", "VS", vs_q)
        ss_qs = em.op("scalar", lambda E, sl=sl: E.activation(
            sq2[:, :], qo[sl][:, :], AF.Square, bias=0.0, scale=1.0), "SS")
        em.wait("vector", "SS", ss_qs)
        em.wait("vector", "GS", gs_init)
        v_h = em.op("vector", lambda E: E.tensor_reduce(
            hn[:, :], sq2[:, :].rearrange("p (h d) -> p h d", h=16), AX.X, ALU.add), "VS")
        em.wait("vector", "VS", v_h)
        qb_i = i // 4
        vs_qn = em.op("vector", lambda E, qb_i=qb_i: E.tensor_tensor(
            qn[:, 16 * qb_i:16 * (qb_i + 1)], qn[:, 16 * qb_i:16 * (qb_i + 1)],
            hn[:, :], ALU.max), "VS")
        em.wait("scalar", "VS", vs_qn)
        ss_ks = em.op("scalar", lambda E, sl=sl: E.activation(
            sq2[:, :], ko[sl][:, :], AF.Square, bias=0.0, scale=1.0), "SS")
        em.wait("vector", "SS", ss_ks)
        em.wait("vector", "VS", vs_qn)
        v_h = em.op("vector", lambda E: E.tensor_reduce(
            hn[:, :], sq2[:, :].rearrange("p (h d) -> p h d", h=16), AX.X, ALU.add), "VS")
        em.wait("vector", "VS", v_h)
        nonlocal_vs = em.op("vector", lambda E: E.tensor_tensor(
            kn[:, :], kn[:, :], hn[:, :], ALU.max), "VS")
        em.wait("sync", "VS", vs_q)
        st_tok["q"][i] = em.dma("sync", "LQ",
                                s_q.rearrange("(t p) d -> t p d", p=128)[i], qo[sl][:, :])
        em.wait("sync", "VS", vs_k)
        st_tok["k"][i] = em.dma("sync", "LQ",
                                s_k.rearrange("(t p) d -> t p d", p=128)[i], ko[sl][:, :])
        em.wait("sync", "VS", vs_v)
        st_tok["v"][i] = em.dma("sync", "LQ",
                                s_v.rearrange("(t p) d -> t p d", p=128)[i], vo[sl][:, :])
        return nonlocal_vs

    for gr in range(9):
        if gr < 8:
            for i in range(4 * gr, 4 * gr + 4):
                emit_lnx(i)
            emit_x1t(gr)
        if gr >= 1:
            for i in range(4 * (gr - 1), 4 * (gr - 1) + 4):
                vs_norm_last = emit_qkv(i)
                if prep_state["done"] < NU:
                    budget = 2 if prep_state["done"] < 32 else 1
                    for _ in range(budget):
                        if prep_state["done"] < NU:
                            prep_unit(prep_state["done"])
                            prep_state["done"] += 1
    while prep_state["done"] < NU:
        prep_unit(prep_state["done"])
        prep_state["done"] += 1

    fctb = wp_in[0][:, :].bitcast(BF16)
    pq_fc_wr = {}
    em.wait("sync", "VS", em.cnt["VS"])
    for c in range(8):
        em.wait("sync", "PQ", prep_state["fc_store"])
        if c >= 1:
            em.wait("sync", "PQ", pq_fc_wr[c - 1])
        lq_t = em.tread("sync", "LQ", fctb, s_fceff[0:FF, 128 * c:128 * (c + 1)])
        em.wait("scalar", "LQ", lq_t)
        pq_fc_wr[c] = em.dma("scalar", "PQ",
                             s_fcT.rearrange("(c p) n -> c p n", p=128)[c], fctb)

    fsc = g["fscB"]

    def vfence():
        return em.op("vector", lambda E: E.tensor_copy(
            fsc[0:1, 0:1], fsc[0:1, 1:2]), "VS")

    for qb in range(NQB):
        vs_c = em.op("vector", lambda E, qb=qb: E.tensor_copy(
            nbtb[:, :], qn[:, 16 * qb:16 * (qb + 1)]), "VS")
        em.wait("vector", "VS", vs_c)
        vs_c = vfence()
        em.wait("tensor", "VS", vs_c)
        ts_t = em.op("tensor", lambda E: E.transpose(
            tp_ps[0:16, 0:128], nbtb[:, :], ident[:, :]), "TS")
        em.wait("vector", "TS", ts_t)
        vs_m = em.op("vector", lambda E, qb=qb: E.tensor_reduce(
            qm2[0:16, qb:qb + 1], tp_ps[0:16, 0:128], AX.X, ALU.max), "VS")
        em.wait("tensor", "VS", vs_m)
    vs_c = em.op("vector", lambda E: E.tensor_copy(nbtb[:, :], kn[:, :]), "VS")
    em.wait("vector", "VS", vs_c)
    vs_c = vfence()
    em.wait("tensor", "VS", vs_c)
    ts_t = em.op("tensor", lambda E: E.transpose(
        tp_ps[0:16, 0:128], nbtb[:, :], ident[:, :]), "TS")
    em.wait("vector", "TS", ts_t)
    vs_km = em.op("vector", lambda E: E.tensor_reduce(
        km2[0:16, 0:1], tp_ps[0:16, 0:128], AX.X, ALU.max), "VS")
    em.wait("scalar", "VS", vs_km)
    s_q1 = em.op("scalar", lambda E: E.activation(
        qm2[0:16, :], qm2[0:16, :], AF.Ln, bias=0.0, scale=1.0), "SS")
    em.wait("scalar", "SS", s_q1)
    ss_l = em.op("scalar", lambda E: E.activation(
        km2[0:16, :], km2[0:16, :], AF.Ln, bias=0.0, scale=1.0), "SS")
    em.wait("scalar", "SS", ss_l)
    ss_l = em.op("scalar", lambda E: E.activation(
        fsc[0:1, 2:3], fsc[0:1, 3:4], AF.Square, bias=0.0, scale=1.0), "SS")
    em.wait("vector", "SS", ss_l)
    vs_s = em.op("vector", lambda E: E.tensor_scalar(
        Bqk[0:16, :], qm2[0:16, :], km2[0:16, 0:1], None, ALU.add), "VS")
    em.wait("vector", "VS", vs_s)
    em.wait("scalar", "VS", vs_s)
    ss_e = em.op("scalar", lambda E: E.activation(
        Bqk[0:16, :], Bqk[0:16, :], AF.Exp, bias=0.0, scale=0.5), "SS")
    em.wait("scalar", "SS", ss_e)
    ss_e = em.op("scalar", lambda E: E.activation(
        fsc[0:1, 2:3], fsc[0:1, 3:4], AF.Square, bias=0.0, scale=1.0), "SS")
    em.wait("vector", "SS", ss_e)
    vs_b = em.op("vector", lambda E: E.tensor_copy(Bqkb[0:16, :], Bqk[0:16, :]), "VS")
    em.wait("vector", "VS", vs_b)
    vs_b = vfence()
    em.wait("tensor", "VS", vs_b)
    ts_t2 = em.op("tensor", lambda E: E.transpose(
        tp_ps[0:8, 0:16], Bqkb[0:16, 0:NQB], ident[0:16, 0:16]), "TS")
    em.wait("vector", "TS", ts_t2)
    v_c = em.op("vector", lambda E: E.tensor_copy(Btq[0:8, :], tp_ps[0:8, 0:16]), "VS")
    em.wait("vector", "VS", v_c)
    v_p = em.op("vector", lambda E: E.tensor_tensor(
        pBn[0:8, :], Btq[0:8, 0:16:2], Btq[0:8, 1:16:2], ALU.max), "VS")
    em.wait("vector", "VS", v_p)
    vs_nb = em.op("vector", lambda E: E.tensor_scalar(
        pBn[0:8, :], pBn[0:8, :], -1.0, SHIFT, ALU.mult, ALU.add), "VS")
    em.wait("scalar", "VS", vs_nb)
    pq_nb = em.dma("scalar", "PQ", s_nb, pBn[0:8, 0:NP])
    em.wait("sync", "PQ", pq_nb)
    lq_nb = em.dma("sync", "LQ", nb164[0:1, :],
                   s_nb.rearrange("a b -> (a b)").rearrange("(o n) -> o n", o=1))
    em.wait("tensor", "LQ", lq_nb)
    em.wait("tensor", "GS", gs_init)
    ts_r = em.op("tensor", lambda E: E.matmul(
        ms_ps[:, 0:NQB * NP], ones32[0:1, 0:128], nb164[0:1, :],
        start=True, stop=True), "TS")
    em.wait("vector", "TS", ts_r)
    vs_nbr = em.op("vector", lambda E: E.tensor_copy(
        negB[:, :], ms_ps[:, 0:NQB * NP]), "VS")

    em.wait("vector", "LQ", lq_gate)
    v_fg = em.op("vector", lambda E: E.tensor_scalar(
        fbg[0:16, :], fb_sb[0:16, :], gate_sb[0:16, 0:1], None, ALU.mult), "VS")
    em.wait("vector", "VS", v_fg)
    vs_f = em.op("vector", lambda E: E.tensor_copy(fbgb[0:16, :], fbg[0:16, :]), "VS")
    em.wait("vector", "VS", vs_f)
    vs_f = vfence()
    em.wait("tensor", "VS", vs_f)
    em.wait("tensor", "VS", vs_nbr)
    em.wait("tensor", "GQ", gq_uT)
    for n in range(8):
        ts_g = em.op("tensor", lambda E, n=n: E.matmul(
            ms_ps[0:16, 0:512], fbgb[0:16, 0:16], uT_sb[0:16, 512 * n:512 * (n + 1)],
            start=True, stop=True), "TS")
        em.wait("vector", "TS", ts_g)
        vs_g = em.op("vector", lambda E, n=n: E.tensor_copy(
            gW2[0:16, 512 * n:512 * (n + 1)], ms_ps[0:16, 0:512]), "VS")
        em.wait("tensor", "VS", vs_g)


def emit_B(em, g):
    (vpr, ktb, qtb, expb, attnT, rcpb, Ssb, fcT, xrt, Asb, ysb, x2b,
     x2t, t1b, t1T, rsb, rrb, lnB, wout_sb, onesb) = (g[k] for k in [
        "vpr", "ktb", "qtb", "expb", "attnT", "rcpb", "Ssb", "fcT", "xrt",
        "Asb", "ysb", "x2b", "x2t", "t1b", "t1T", "rsb", "rrb", "lnB",
        "wout_sb", "onesb"])
    sc_ps, pvp, sums_ps, msB_ps = (
        g["sc_ps"], g["pvp"], g["sums_ps"], g["msB_ps"])
    (ident, ones32, negB, bout_sb, gW2, vT) = (g[k] for k in [
        "ident", "ones32", "negB", "bout_sb", "gW2", "vT"])
    (x_in, wout_in, s_q, s_k, s_v, s_fcT, s_y, s_rr) = (g[k] for k in [
        "x_in", "wout_in", "s_q", "s_k", "s_v", "s_fcT", "s_y", "s_rr"])

    x_tok = x_in.rearrange("(t p) d -> t p d", p=128)
    y_tok = s_y.rearrange("(t p) d -> t p d", p=128)
    fcT_src = s_fcT.rearrange("(c p) f -> p c f", p=128)

    st = {
        "qt_free": {0: 0, 1: 0},
        "exp_free": {0: 0, 1: 0, 2: 0, 3: 0}, "last_exp": 0, "pv_free": 0,
        "ms": 0, "tp": 0,
        "fcT_free": {0: 0, 1: 0}, "x_free": {0: 0, 1: 0},
        "y_free": {0: 0, 1: 0}, "x2_free": 0, "rr_free": 0,
        "A_free": {0: 0, 1: 0}, "ysq_free": {0: 0, 1: 0},
        "rsb_free": {0: 0, 1: 0}, "slot_free": {},
        "x2t_free": 0, "s_last": 0,
        "kq_free": {0: 0, 1: 0, 2: 0},
        "last_rc": 0, "last_exp2": {0: 0, 1: 0},
    }
    kq_load, qt_load, fcT_load, x_load = {}, {}, {}, {}
    defer_q = []
    epi_q = []
    fsc = g["fscB"]

    def vfence():
        return em.op("vector", lambda E: E.tensor_copy(
            fsc[0:1, 0:1], fsc[0:1, 1:2]), "VS")
    misc_q = []

    def misc_pop(n=1):
        for _ in range(n):
            if misc_q:
                misc_q.pop(0)()

    gq_wo = em.dma("gpsimd", "GQ",
                   wout_sb[:, :].rearrange("p (c n) -> p c n", c=8),
                   wout_in.rearrange("(c p) n -> p c n", p=128))
    sv3 = s_v.rearrange("(t p) d -> p t d", p=128)
    vpr3 = vpr[:, :].rearrange("p (t d) -> p t d", t=NT)
    lq_v = 0
    for h in range(4):
        lq_v = em.dma("sync", "LQ", vpr3[:, 8 * h:8 * (h + 1), :],
                      sv3[:, 8 * h:8 * (h + 1), :])
    gs_ob = em.op("gpsimd", lambda E: E.memset(onesb[:, :], 1.0), "GS")

    def kq_slot(qb, p_i, qt):
        return ((qb * NP + p_i) * 4 + qt) % 3

    def kq_prefetch(qb, p_i, qt):
        if qb >= NQB or (qb, p_i, qt) in kq_load:
            return
        sslot = kq_slot(qb, p_i, qt)
        em.wait("sync", "TS", st["kq_free"][sslot])
        kq_load[(qb, p_i, qt)] = em.tread(
            "sync", "LQ", ktb[sslot][:, :],
            s_k[1024 * qt:1024 * (qt + 1), 128 * p_i:128 * (p_i + 1)])

    def qt_prefetch(qb, p_i):
        if qb >= NQB or (qb, p_i) in qt_load:
            return
        nsl = p_i % 2
        em.wait("sync", "TS", st["qt_free"][nsl])
        qt_load[(qb, p_i)] = em.tread(
            "sync", "LQ", qtb[nsl][:, :],
            s_q[qb * QB:(qb + 1) * QB, 128 * p_i:128 * (p_i + 1)])

    kq_prefetch(0, 0, 0)
    kq_prefetch(0, 0, 1)
    qt_prefetch(0, 0)

    def make_misc(qb):
        slot = lambda pp: (qb * 8 + pp) % 10
        loc = {"x2t": {}, "t1": {}}

        def g_fcload(n):
            def go():
                fsl = n % 2
                em.wait("sync", "TS", st["fcT_free"][fsl])
                fcT_load[(qb, n)] = em.dma(
                    "sync", "LQ", fcT[fsl][:, :], fcT_src[:, :, 512 * n:512 * (n + 1)])
            return go

        def g_xload(m):
            def go():
                em.wait("sync", "VS", st["x_free"][m % 2])
                x_load[(qb, m)] = em.dma("sync", "LQ", xrt[m % 2][:, :],
                                         x_tok[qb * 4 + m])
            return go

        def g_w(m, n):
            def go():
                em.wait("tensor", "VS", st["pv_free"])
                if n == 0:
                    em.wait("vector", "SS", st["A_free"][m % 2])
                em.wait("tensor", "VS", st["ms"])
                em.wait("tensor", "GQ", gq_wo)
                ts_w = None
                for c in range(8):
                    ts_w = em.op("tensor", lambda E, c=c, m=m, n=n: E.matmul(
                        msB_ps[:, :],
                        attnT[:, slot(c) * 512 + 128 * m: slot(c) * 512 + 128 * (m + 1)],
                        wout_sb[:, 1024 * c + 512 * n: 1024 * c + 512 * (n + 1)],
                        start=(c == 0), stop=(c == 7)),
                        "TS" if c == 7 else None)
                if m == 3 and n == 1:
                    for c in range(8):
                        st["slot_free"][slot(c)] = ts_w
                em.wait("vector", "TS", ts_w)
                st["ms"] = em.op("vector", lambda E, m=m, n=n: E.tensor_tensor(
                    Asb[m % 2][:, 512 * n:512 * (n + 1)], msB_ps[:, :],
                    bout_sb[:, 512 * n:512 * (n + 1)], ALU.add), "VS")
            return go

        def g_lnstatA(m):
            def go():
                fv = vfence()
                em.wait("scalar", "VS", fv)
                if g.get("debug"):
                    em.dma("scalar", "PQ",
                           g["s_A"].rearrange("(t p) d -> t p d", p=128)[qb * 4 + m],
                           Asb[m % 2][:, :])
                loc[("sA", m)] = ln_stat(em, Asb[m % 2][:, :], D, lnB[m % 2])
                st["A_free"][m % 2] = loc[("sA", m)]
            return go

        def g_lnfinA(m):
            def go():
                loc[("fA", m)] = ln_fin(em, D, lnB[m % 2], loc[("sA", m)])
            return go

        def g_applyA(m):
            def go():
                vs_ln = ln_apply(em, Asb[m % 2][:, :], lnB[m % 2], loc[("fA", m)],
                                 Asb[m % 2][:, :])
                em.wait("vector", "VS", vs_ln)
                em.wait("vector", "LQ", x_load[(qb, m)])
                em.wait("vector", "PQ", st["y_free"][m % 2])
                em.wait("vector", "SS", st["ysq_free"][m % 2])
                vs_y = em.op("vector", lambda E, m=m: E.tensor_tensor(
                    ysb[m % 2][:, :], Asb[m % 2][:, :], xrt[m % 2][:, :], ALU.add), "VS")
                st["x_free"][m % 2] = vs_y
                em.wait("vector", "VS", vs_y)
                fv = vfence()
                em.wait("scalar", "VS", fv)
                st["y_free"][m % 2] = em.dma("scalar", "PQ", y_tok[qb * 4 + m],
                                             ysb[m % 2][:, :])
                loc[("y", m)] = vs_y
            return go

        def g_lnstatY(m):
            def go():
                em.wait("vector", "VS", loc[("y", m)])
                loc[("sY", m)] = ln_stat(em, ysb[m % 2][:, :], D, lnB[m % 2])
                st["ysq_free"][m % 2] = loc[("sY", m)]
            return go

        def g_lnfinY(m):
            def go():
                loc[("fY", m)] = ln_fin(em, D, lnB[m % 2], loc[("sY", m)])
            return go

        def g_applyY(m):
            def go():
                em.wait("vector", "PQ", st["x2_free"])
                vs_x2 = ln_apply(em, ysb[m % 2][:, :], lnB[m % 2], loc[("fY", m)],
                                 x2b[:, :])
                em.wait("vector", "VS", vs_x2)
                fv = vfence()
                em.wait("scalar", "VS", fv)
                st["x2_free"] = em.dma(
                    "scalar", "PQ",
                    g["s_x2"].rearrange("(t p) d -> t p d", p=128)[qb * 4 + m],
                    x2b[:, :])
                loc["x2sp"] = st["x2_free"]
            return go

        def g_x2t():
            def go():
                em.wait("sync", "PQ", loc["x2sp"])
                em.wait("sync", "TS", st["x2t_free"])
                lq = 0
                for c in range(8):
                    lq = em.tread("sync", "LQ", x2t[:, 512 * c:512 * (c + 1)],
                                  g["s_x2"][QB * qb:QB * (qb + 1), 128 * c:128 * (c + 1)])
                loc["x2t_ld"] = lq
            return go

        def g_t1T():
            def go():
                em.wait("tensor", "LQ", loc["x2t_ld"])
                em.wait("tensor", "VS", st["ms"])
                em.wait("tensor", "GQ", gq_wo)
                ts_1 = None
                for c in range(8):
                    ts_1 = em.op("tensor", lambda E, c=c: E.matmul(
                        msB_ps[0:16, :], vT[:, 16 * c:16 * (c + 1)],
                        x2t[:, 512 * c:512 * (c + 1)],
                        start=(c == 0), stop=(c == 7)), "TS" if c == 7 else None)
                em.wait("vector", "TS", ts_1)
                st["ms"] = em.op("vector", lambda E: E.tensor_copy(
                    t1T[0:16, :], msB_ps[0:16, :]), "VS")
            return go

        def g_fc(n, m):
            def go():
                fsl = n % 2
                em.wait("tensor", "LQ", fcT_load[(qb, n)])
                em.wait("tensor", "LQ", loc["x2t_ld"])
                em.wait("tensor", "VS", st["ms"])
                for c in range(8):
                    em.op("tensor", lambda E, c=c, m=m, fsl=fsl: E.matmul(
                        msB_ps[:, :],
                        x2t[:, 512 * c + 128 * m: 512 * c + 128 * (m + 1)],
                        fcT[fsl][:, 512 * c:512 * (c + 1)],
                        start=(c == 0), stop=False))
                ts_f = em.op("tensor", lambda E, m=m, n=n: E.matmul(
                    msB_ps[:, :], t1T[0:16, 128 * m:128 * (m + 1)],
                    gW2[0:16, 512 * n:512 * (n + 1)], start=False, stop=True), "TS")
                if m == 3:
                    st["fcT_free"][fsl] = ts_f
                    if n == 7:
                        st["x2t_free"] = ts_f
                    nn = n + 2
                    if nn < 8:
                        g_fcload(nn)()
                em.wait("vector", "TS", ts_f)
                sl_r = (n * 4 + m) % 2
                em.wait("vector", "GS", st["rsb_free"][sl_r])
                vs_r = em.op("vector", lambda E, sl_r=sl_r: E.tensor_scalar(
                    rsb[sl_r][:, :], msB_ps[:, :], 0.0, None, ALU.max), "VS")
                st["ms"] = vs_r
                em.wait("gpsimd", "VS", vs_r)
                if m == 0:
                    em.wait("gpsimd", "PQ", st["rr_free"])
                gs_r = em.op("gpsimd", lambda E, sl_r=sl_r, m=m: E.tensor_tensor(
                    rrb[:, 512 * m:512 * (m + 1)], rsb[sl_r][:, :],
                    rsb[sl_r][:, :], ALU.mult), "GS")
                st["rsb_free"][sl_r] = gs_r
                if m == 3:
                    em.wait("scalar", "GS", gs_r)
                    st["rr_free"] = em.dma(
                        "scalar", "PQ",
                        s_rr[qb * 512:(qb + 1) * 512, 512 * n:512 * (n + 1)]
                        .rearrange("(mm p) f -> p mm f", p=128),
                        rrb[:, :].rearrange("p (mm f) -> p mm f", mm=4))
            return go

        seq = [g_fcload(0), g_fcload(1)]
        for m in range(4):
            seq.append(g_xload(m))
            seq.append(g_w(m, 0))
            seq.append(g_w(m, 1))
            seq.append(g_lnstatA(m))
            seq.append(g_lnfinA(m))
            seq.append(g_applyA(m))
            seq.append(g_lnstatY(m))
            seq.append(g_lnfinY(m))
            seq.append(g_applyY(m))
        seq.append(g_x2t())
        seq.append(g_t1T())
        for n in range(8):
            for m in range(4):
                seq.append(g_fc(n, m))
        return seq

    for qb in range(NQB):
        for p_i in range(NP):
            sl = p_i % 2
            qt_prefetch(qb, p_i)
            nq, npair = (qb, p_i + 1) if p_i + 1 < NP else (qb + 1, 0)
            qt_prefetch(nq, npair)
            em.wait("tensor", "LQ", qt_load[(qb, p_i)])
            em.wait("tensor", "LQ", lq_v)
            col = qb * 8 + p_i
            jA = (2 * p_i) % 4
            jB = (2 * p_i + 1) % 4
            for cc in range(32):
                qt_i = cc // 8
                if cc % 8 == 0:
                    for dg in (1, 2):
                        qt_n = qt_i + dg
                        tgt = (qb, p_i, qt_n) if qt_n < 4 else (nq, npair, qt_n - 4)
                        if tgt[0] < NQB:
                            kq_prefetch(*tgt)
                sslot = kq_slot(qb, p_i, qt_i)
                em.wait("tensor", "LQ", kq_load[(qb, p_i, qt_i)])
                em.wait("tensor", "SS", st["last_exp2"][cc % 2])
                soff = 1024 * (cc % 2)
                ko2 = 128 * (cc - 8 * qt_i)
                em.op("tensor", lambda E, ko2=ko2, soff=soff, sslot=sslot, sl=sl: E.matmul(
                    sc_ps[:, soff:soff + 512],
                    ktb[sslot][0:64, ko2:ko2 + 128],
                    qtb[sl][0:64, :], start=True, stop=True))
                ts_sc = em.op("tensor", lambda E, ko2=ko2, soff=soff, sslot=sslot, sl=sl: E.matmul(
                    sc_ps[:, soff + 512:soff + 1024],
                    ktb[sslot][64:128, ko2:ko2 + 128],
                    qtb[sl][64:128, :], start=True, stop=True), "TS")
                if cc % 8 == 7:
                    st["kq_free"][sslot] = ts_sc
                esl = cc % 4
                em.wait("scalar", "TS", ts_sc)
                em.wait("scalar", "TS", st["exp_free"][esl])
                ss_e = em.op("scalar", lambda E, esl=esl, soff=soff, col=col: E.activation(
                    expb[esl][:, :], sc_ps[:, soff:soff + 1024], AF.Exp,
                    bias=negB[:, col:col + 1], scale=1.0), "SS")
                st["last_exp2"][cc % 2] = ss_e

                def mk_pv(cc=cc, esl=esl, ss_e=ss_e, p_i=p_i, jA=jA, jB=jB):
                    def go_pv():
                        em.wait("tensor", "SS", ss_e)
                        if cc == 0:
                            em.wait("tensor", "VS", st["pv_free"])
                            em.wait("tensor", "VS", st["last_rc"])
                        em.op("tensor",
                              lambda E, cc=cc, p_i=p_i, esl=esl: E.matmul(
                                  pvp[p_i % 2][0:64, :],
                                  vpr[:, cc * D + (2 * p_i) * 64: cc * D + (2 * p_i) * 64 + 64],
                                  expb[esl][:, 0:512],
                                  start=(cc == 0), stop=(cc == 31)))
                        em.op("tensor",
                              lambda E, cc=cc, p_i=p_i, esl=esl: E.matmul(
                                  pvp[p_i % 2][64:128, :],
                                  vpr[:, cc * D + (2 * p_i + 1) * 64: cc * D + (2 * p_i + 1) * 64 + 64],
                                  expb[esl][:, 512:1024],
                                  start=(cc == 0), stop=(cc == 31),
                                  tile_position=(0, 64)))
                        em.op("tensor", lambda E, cc=cc, jA=jA, esl=esl: E.matmul(
                            sums_ps[32 * jA:32 * jA + 1, :], onesb[:, :],
                            expb[esl][:, 0:512],
                            start=(cc == 0), stop=(cc == 31),
                            tile_position=(0, 32 * jA)))
                        ts_pv = em.op("tensor", lambda E, cc=cc, jB=jB, esl=esl: E.matmul(
                            sums_ps[32 * jB:32 * jB + 1, :], onesb[:, :],
                            expb[esl][:, 512:1024],
                            start=(cc == 0), stop=(cc == 31),
                            tile_position=(0, 32 * jB)), "TS")
                        st["exp_free"][esl] = ts_pv
                    return go_pv

                defer_q.append(mk_pv())
                while len(defer_q) > 3:
                    defer_q.pop(0)()
                if cc == 6 and epi_q:
                    epi_q.pop(0)()
                if cc % 3 == 1:
                    misc_pop(1)
            st["qt_free"][sl] = em.cnt["TS"]
            def mk_epi_ve(p_i=p_i, jA=jA, jB=jB):
                def go():
                    em.wait("vector", "TS", st["exp_free"][0])
                    em.wait("vector", "TS", st["exp_free"][1])
                    em.wait("vector", "TS", st["exp_free"][2])
                    em.wait("vector", "TS", st["exp_free"][3])
                    em.wait("vector", "TS", st["s_last"])
                    v_r1 = em.op("vector", lambda E, jA=jA: E.reciprocal(
                        rcpb[32 * jA:32 * jA + 1, :], sums_ps[32 * jA:32 * jA + 1, :]), "VS")
                    em.wait("vector", "VS", v_r1)
                    vs_rc = em.op("vector", lambda E, jB=jB: E.reciprocal(
                        rcpb[32 * jB:32 * jB + 1, :], sums_ps[32 * jB:32 * jB + 1, :]), "VS")
                    em.wait("vector", "VS", vs_rc)
                    st["last_rc"] = vfence()
                return go

            def mk_epi_te(p_i=p_i, qb=qb, jA=jA, jB=jB):
                def go():
                    em.wait("tensor", "VS", st["last_rc"])
                    em.wait("tensor", "VS", st["ms"])
                    em.op("tensor", lambda E, jA=jA: E.matmul(
                        msB_ps[0:64, :], ones32[32 * jA:32 * jA + 1, 0:64],
                        rcpb[32 * jA:32 * jA + 1, :], start=True, stop=True,
                        tile_position=(32 * jA, 0)))
                    ts_s = em.op("tensor", lambda E, jB=jB: E.matmul(
                        msB_ps[64:128, :], ones32[32 * jB:32 * jB + 1, 0:64],
                        rcpb[32 * jB:32 * jB + 1, :], start=True, stop=True,
                        tile_position=(32 * jB, 64)), "TS")
                    st["s_last"] = ts_s
                    em.wait("vector", "TS", ts_s)
                    st["ms"] = em.op("vector", lambda E: E.tensor_copy(
                        Ssb[:, :], msB_ps[:, :]), "VS")
                    em.wait("vector", "VS", st["ms"])
                    pslot = (qb * 8 + p_i) % 10
                    em.wait("vector", "TS", st["slot_free"].get(pslot, 0))
                    st["pv_free"] = em.op("vector", lambda E, pslot=pslot, p_i=p_i: E.tensor_tensor(
                        attnT[:, pslot * 512:(pslot + 1) * 512], pvp[p_i % 2][:, :],
                        Ssb[:, :], ALU.mult), "VS")
                    if g.get("debug"):
                        em.wait("scalar", "VS", st["pv_free"])
                        em.dma("scalar", "PQ",
                               g["s_attn"][(qb * 8 + p_i) * 128:(qb * 8 + p_i + 1) * 128, :],
                               attnT[:, pslot * 512:(pslot + 1) * 512])
                return go

            defer_q.append(mk_epi_ve())
            epi_q.append(mk_epi_te())
        misc_q.extend(make_misc(qb))
    while defer_q:
        defer_q.pop(0)()
    while epi_q:
        epi_q.pop(0)()
    misc_pop(len(misc_q))


def emit_D(em, g):
    (pjT, rrT, yD, fD, oD, lnD, pj_ps) = (g[k] for k in [
        "pjT", "rrT", "yD", "fD", "oD", "lnD", "pj_ps"])
    s_pjeff, s_rr, s_y, out = g["s_pjeff"], g["s_rr"], g["s_y"], g["out"]

    lq_pj = 0
    for k in range(32):
        lq_pj = em.tread("sync", "LQ", pjT[:, 1024 * k:1024 * (k + 1)],
                         s_pjeff[0:D, 128 * k:128 * (k + 1)])
    rr_load, y_load = {}, {}
    fsc = g["fscB"]
    last_sq_ss = [0]
    rr_free = {0: 0, 1: 0}
    o_free = {0: 0, 1: 0}
    y_free = {0: 0, 1: 0}
    pj_free = {i: 0 for i in range(4)}

    def load_slab(sl):
        ssl = sl % 2
        em.wait("sync", "TS", rr_free[ssl])
        lq = 0
        for k in range(32):
            lq = em.tread("sync", "LQ", rrT[ssl][:, 512 * k:512 * (k + 1)],
                          s_rr[sl * 512:(sl + 1) * 512, 128 * k:128 * (k + 1)])
        rr_load[sl] = lq

    load_slab(0)
    load_slab(1)
    y_tok = s_y.rearrange("(t p) d -> t p d", p=128)
    o_tok = out.rearrange("(t p) d -> t p d", p=128)

    def y_prefetch(sl, m):
        if sl >= NQB or m >= 4 or (sl, m) in y_load:
            return
        em.wait("sync", "VS", y_free[m % 2])
        y_load[(sl, m)] = em.dma("sync", "LQ", yD[m % 2][:, :], y_tok[sl * 4 + m])

    y_prefetch(0, 0)
    y_prefetch(0, 1)
    for sl in range(NQB):
        ssl = sl % 2
        em.wait("tensor", "LQ", rr_load[sl])
        em.wait("tensor", "LQ", lq_pj)
        for m in range(4):
            last_ts = {}
            for n in range(2):
                pb = (m * 2 + n) % 4
                em.wait("tensor", "VS", pj_free[pb])
                for k in range(32):
                    tok = em.op("tensor", lambda E, m=m, n=n, k=k, pb=pb, ssl=ssl: E.matmul(
                        pj_ps[pb][:, :],
                        rrT[ssl][:, 512 * k + 128 * m: 512 * k + 128 * (m + 1)],
                        pjT[:, 1024 * k + 512 * n: 1024 * k + 512 * (n + 1)],
                        start=(k == 0), stop=(k == 31)),
                        "TS" if k == 31 else None)
                last_ts[n] = tok
            for n in range(2):
                pb = (m * 2 + n) % 4
                em.wait("vector", "TS", last_ts[n])
                em.wait("vector", "SS", last_sq_ss[0])
                pj_free[pb] = em.op("vector", lambda E, n=n, pb=pb: E.tensor_copy(
                    fD[:, 512 * n:512 * (n + 1)], pj_ps[pb][:, :]), "VS")
            em.wait("vector", "VS", pj_free[(m * 2 + 1) % 4])
            fv = em.op("vector", lambda E: E.tensor_copy(
                fsc[0:1, 0:1], fsc[0:1, 1:2]), "VS")
            em.wait("scalar", "VS", fv)
            if g.get("debug"):
                em.dma("scalar", "PQ",
                       g["s_F"].rearrange("(t p) d -> t p d", p=128)[sl * 4 + m],
                       fD[:, :])
            osl = m % 2
            em.wait("vector", "PQ", o_free[osl])
            vs_lnf, ss_lnf = emit_ln(em, fD[:, :], D, lnD, oD[osl][:, :])
            last_sq_ss[0] = ss_lnf
            em.wait("vector", "VS", vs_lnf)
            em.wait("vector", "LQ", y_load[(sl, m)])
            vs_o = em.op("vector", lambda E, osl=osl, m=m: E.tensor_tensor(
                oD[osl][:, :], oD[osl][:, :], yD[m % 2][:, :], ALU.add), "VS")
            y_free[m % 2] = vs_o
            em.wait("scalar", "VS", vs_o)
            o_free[osl] = em.dma("scalar", "PQ", o_tok[sl * 4 + m], oD[osl][:, :])
            if m + 2 < 4:
                y_prefetch(sl, m + 2)
            else:
                y_prefetch(sl + 1, m - 2)
        rr_free[ssl] = em.cnt["TS"]
        if sl + 2 < NQB:
            load_slab(sl + 2)
    em.wait("sync", "PQ", em.cnt["PQ"])
    em.wait("sync", "LQ", em.cnt["LQ"])
    em.wait("sync", "GQ", em.cnt["GQ"])
    em.wait("sync", "GS", em.cnt["GS"])


def build(debug=False, phases=("A", "B", "D")):
    nc = bass.Bass("TRN2", target_bir_lowering=False)
    em = Em(nc)

    tensors = {}
    t = tensors
    t["x_in"] = nc.dram_tensor("x", [L, D], F32, kind="ExternalInput").ap()
    t["wqkv"] = nc.dram_tensor("wqkv", [D, 3 * D], F32, kind="ExternalInput").ap()
    t["bqkv_r"] = nc.dram_tensor("bqkv_r", [128, 3 * D], F32, kind="ExternalInput").ap()
    t["wout_in"] = nc.dram_tensor("wout", [D, D], F32, kind="ExternalInput").ap()
    t["bout_r"] = nc.dram_tensor("bout_r", [128, D], F32, kind="ExternalInput").ap()
    t["wfc_in"] = nc.dram_tensor("wfc", [FF, D], F32, kind="ExternalInput").ap()
    t["wpj_in"] = nc.dram_tensor("wpj", [D, FF], F32, kind="ExternalInput").ap()
    t["uT_in"] = nc.dram_tensor("uT", [R, FF], F32, kind="ExternalInput").ap()
    t["vT_in"] = nc.dram_tensor("vTt", [128, 128], F32, kind="ExternalInput").ap()
    t["fb_in"] = nc.dram_tensor("fb", [R, R], F32, kind="ExternalInput").ap()
    t["gate_r"] = nc.dram_tensor("gate_r", [R, 1], F32, kind="ExternalInput").ap()
    t["embq_in"] = nc.dram_tensor("embq", [128, NT * HD], F32, kind="ExternalInput").ap()
    t["embk_in"] = nc.dram_tensor("embk", [128, NT * HD], F32, kind="ExternalInput").ap()
    t["ident_in"] = nc.dram_tensor("ident", [128, 128], F32, kind="ExternalInput").ap()
    if "D" in phases:
        t["out"] = nc.dram_tensor("out", [L, D], F32, kind="ExternalOutput").ap()

    ik = "ExternalOutput" if debug else "Internal"
    t["s_q"] = nc.dram_tensor("s_q", [L, D], BF16, kind=ik).ap()
    t["s_k"] = nc.dram_tensor("s_k", [L, D], BF16, kind=ik).ap()
    t["s_v"] = nc.dram_tensor("s_v", [L, VW], BF16, kind=ik).ap()
    t["s_fceff"] = nc.dram_tensor("s_fceff", [FF, D], BF16, kind=ik).ap()
    t["s_fcT"] = nc.dram_tensor("s_fcT", [D, FF], BF16, kind=ik).ap()
    t["s_pjeff"] = nc.dram_tensor("s_pjeff", [D, FF], BF16, kind=ik).ap()
    t["s_y"] = nc.dram_tensor("s_y", [L, D], F32, kind=ik).ap()
    t["s_rr"] = nc.dram_tensor("s_rr", [L, FF], BF16, kind=ik).ap()
    t["s_nb"] = nc.dram_tensor("s_nb", [NQB, NP], F32, kind=ik).ap()
    t["s_attn"] = nc.dram_tensor("s_attn", [NQB * NP * 128, 512], BF16, kind=ik).ap() if debug else None
    t["s_A"] = nc.dram_tensor("s_A", [L, D], F32, kind=ik).ap() if debug else None
    t["s_F"] = nc.dram_tensor("s_F", [L, D], F32, kind=ik).ap() if debug else None
    t["s_x1"] = nc.dram_tensor("s_x1", [L, D], BF16, kind=ik).ap()
    t["s_x2"] = nc.dram_tensor("s_x2", [L, D], BF16, kind=ik).ap()
    t["debug"] = debug

    with ExitStack() as es:
        for nm in ["LQ", "PQ", "GQ", "TS", "VS", "SS", "GS"]:
            em.add_sem(nm, es.enter_context(nc.semaphore(nm)))

        P = dict(side="right")
        t["ident"] = es.enter_context(nc.sbuf_tensor("identb", [128, 128], BF16, **P))
        t["fscB"] = es.enter_context(nc.sbuf_tensor("fsc", [1, 4], F32, **P))
        t["ones32"] = es.enter_context(nc.sbuf_tensor("ones32", [128, 128], F32, **P))
        t["negB"] = es.enter_context(nc.sbuf_tensor("negBr", [128, NQB * NP], F32, **P))
        t["bout_sb"] = es.enter_context(nc.sbuf_tensor("bout_sb", [128, D], F32, **P))
        t["gW2"] = es.enter_context(nc.sbuf_tensor("gW2", [16, FF], BF16, **P))
        t["vT"] = es.enter_context(nc.sbuf_tensor("vT", [128, 8 * R], BF16, **P))

        if "A" in phases:
            with ExitStack() as ea:
                a = dict(t)
                a["wq_sb"] = ea.enter_context(nc.sbuf_tensor("wq_sb", [128, 8 * 3 * D], BF16))
                a["embq_sb"] = ea.enter_context(nc.sbuf_tensor("embq_sb", [128, NT * HD], F32))
                a["embk_sb"] = ea.enter_context(nc.sbuf_tensor("embk_sb", [128, NT * HD], F32))
                a["bqkv_sb"] = ea.enter_context(nc.sbuf_tensor("bqkv_sb", [128, 3 * D], F32))
                a["xt"] = [ea.enter_context(nc.sbuf_tensor(f"xt{i}", [128, D], F32)) for i in range(2)]
                a["x1b"] = [ea.enter_context(nc.sbuf_tensor(f"x1b{i}", [128, D], BF16)) for i in range(2)]
                a["x1t"] = [ea.enter_context(nc.sbuf_tensor(f"x1t{i}", [128, 8 * 512], BF16)) for i in range(2)]
                a["qkvb"] = ea.enter_context(nc.sbuf_tensor("qkvb", [128, 3 * D], F32))
                a["rtmp"] = ea.enter_context(nc.sbuf_tensor("rtmp", [128, D], F32))
                a["qo"] = [ea.enter_context(nc.sbuf_tensor(f"qo{i}", [128, D], BF16)) for i in range(2)]
                a["ko"] = [ea.enter_context(nc.sbuf_tensor(f"ko{i}", [128, D], BF16)) for i in range(2)]
                a["vo"] = [ea.enter_context(nc.sbuf_tensor(f"vo{i}", [128, VW], BF16)) for i in range(2)]
                a["lnA"] = ln_scratch(ea, nc, "lnA", D)
                a["hn"] = ea.enter_context(nc.sbuf_tensor("hn", [128, H], F32))
                a["qn"] = ea.enter_context(nc.sbuf_tensor("qn", [128, NQB * H], F32))
                a["kn"] = ea.enter_context(nc.sbuf_tensor("kn", [128, H], F32))
                a["nbtb"] = ea.enter_context(nc.sbuf_tensor("nbtb", [128, H], BF16))
                a["qm2"] = ea.enter_context(nc.sbuf_tensor("qm2", [16, NQB], F32))
                a["km2"] = ea.enter_context(nc.sbuf_tensor("km2", [16, 1], F32))
                a["Bqk"] = ea.enter_context(nc.sbuf_tensor("Bqk", [16, NQB], F32))
                a["Bqkb"] = ea.enter_context(nc.sbuf_tensor("Bqkb", [16, NQB], BF16))
                a["Btq"] = ea.enter_context(nc.sbuf_tensor("Btq", [8, 16], F32))
                a["pBn"] = ea.enter_context(nc.sbuf_tensor("pBn", [8, NP], F32))
                a["nb164"] = ea.enter_context(nc.sbuf_tensor("nb164", [1, NQB * NP], F32))
                a["wp_in"] = [ea.enter_context(nc.sbuf_tensor(f"wp_in{i}", [128, 2048], F32)) for i in range(2)]
                a["wp_t"] = ea.enter_context(nc.sbuf_tensor("wp_t", [128, 2048], F32))
                a["wp_g"] = ea.enter_context(nc.sbuf_tensor("wp_g", [128, 2048], F32))
                a["wp_o"] = [ea.enter_context(nc.sbuf_tensor(f"wp_o{i}", [128, 2048], BF16)) for i in range(2)]
                a["wp_s"] = ea.enter_context(nc.sbuf_tensor("wp_s", [128, 4], F32))
                a["uT_sb"] = ea.enter_context(nc.sbuf_tensor("uT_sb", [16, FF], BF16))
                a["fb_sb"] = ea.enter_context(nc.sbuf_tensor("fb_sb", [16, R], F32))
                a["gate_sb"] = ea.enter_context(nc.sbuf_tensor("gate_sb", [16, 1], F32))
                a["fbg"] = ea.enter_context(nc.sbuf_tensor("fbg", [16, R], F32))
                a["fbgb"] = ea.enter_context(nc.sbuf_tensor("fbgb", [16, R], BF16))
                a["qkv_ps"] = ea.enter_context(nc.psum_tensor("qkv_ps", [128, 3 * D], F32))
                a["tp_ps"] = ea.enter_context(nc.psum_tensor("tp_ps", [128, 1024], BF16))
                a["ms_ps"] = ea.enter_context(nc.psum_tensor("ms_ps", [128, 512], F32))
                em.phase()
                c0 = em.barrier_counts()
                emit_A(em, a)
                em.check_deadlock(c0)
                with nc.Block("phA") as blk:
                    em.replay(blk)
        cnts_A = em.barrier_counts()

        if "B" in phases:
            with ExitStack() as eb:
                b = dict(t)
                b["vpr"] = eb.enter_context(nc.sbuf_tensor("vpr", [128, NT * VW], BF16))
                b["wout_sb"] = eb.enter_context(nc.sbuf_tensor("wout_sb", [128, 8 * D], BF16))
                b["ktb"] = [eb.enter_context(nc.sbuf_tensor(f"ktb{i}", [128, 1024], BF16)) for i in range(3)]
                b["qtb"] = [eb.enter_context(nc.sbuf_tensor(f"qtb{i}", [128, QB], BF16)) for i in range(2)]
                b["expb"] = [eb.enter_context(nc.sbuf_tensor(f"expb{i}", [128, 1024], BF16)) for i in range(4)]
                b["attnT"] = eb.enter_context(nc.sbuf_tensor("attnT", [128, 10 * 512], BF16))
                b["rcpb"] = eb.enter_context(nc.sbuf_tensor("rcpb", [128, 512], F32))
                b["Ssb"] = eb.enter_context(nc.sbuf_tensor("Ssb", [128, 512], F32))
                b["onesb"] = eb.enter_context(nc.sbuf_tensor("onesb", [128, 1], BF16))
                b["fcT"] = [eb.enter_context(nc.sbuf_tensor(f"fcTb{i}", [128, 8 * 512], BF16)) for i in range(2)]
                b["xrt"] = [eb.enter_context(nc.sbuf_tensor(f"xrt{i}", [128, D], F32)) for i in range(2)]
                b["Asb"] = [eb.enter_context(nc.sbuf_tensor(f"Asb{i}", [128, D], F32)) for i in range(2)]
                b["ysb"] = [eb.enter_context(nc.sbuf_tensor(f"ysb{i}", [128, D], F32)) for i in range(2)]
                b["x2b"] = eb.enter_context(nc.sbuf_tensor("x2b", [128, D], BF16))
                b["x2t"] = eb.enter_context(nc.sbuf_tensor("x2t", [128, 8 * QB], BF16))
                b["t1b"] = eb.enter_context(nc.sbuf_tensor("t1b", [128, R], BF16))
                b["t1T"] = eb.enter_context(nc.sbuf_tensor("t1T", [16, QB], BF16))
                b["rsb"] = [eb.enter_context(nc.sbuf_tensor(f"rsb{i}", [128, 512], F32)) for i in range(2)]
                b["rrb"] = eb.enter_context(nc.sbuf_tensor("rrb", [128, 4 * 512], BF16))
                b["lnB"] = [ln_scratch(eb, nc, f"lnB{i}", D) for i in range(2)]
                b["sc_ps"] = eb.enter_context(nc.psum_tensor("sc_ps", [128, 2048], F32))
                b["pvp"] = [eb.enter_context(nc.psum_tensor(f"pvp{i}", [128, 512], F32)) for i in range(2)]
                b["sums_ps"] = eb.enter_context(nc.psum_tensor("sums_ps", [128, 512], F32))
                b["msB_ps"] = eb.enter_context(nc.psum_tensor("msB_ps", [128, 512], F32))
                em.phase()
                c0 = em.barrier_counts()
                em.wait_all(cnts_A)
                emit_B(em, b)
                em.check_deadlock(c0)
                with nc.Block("phB") as blk:
                    em.replay(blk)
        cnts_B = em.barrier_counts()

        if "D" in phases:
            with ExitStack() as ed:
                d = dict(t)
                d["pjT"] = ed.enter_context(nc.sbuf_tensor("pjT", [128, 32 * D], BF16))
                d["rrT"] = [ed.enter_context(nc.sbuf_tensor(f"rrT{i}", [128, 32 * 512], BF16)) for i in range(2)]
                d["yD"] = [ed.enter_context(nc.sbuf_tensor(f"yD{i}", [128, D], F32)) for i in range(2)]
                d["fD"] = ed.enter_context(nc.sbuf_tensor("fD", [128, D], F32))
                d["oD"] = [ed.enter_context(nc.sbuf_tensor(f"oD{i}", [128, D], F32)) for i in range(2)]
                d["lnD"] = ln_scratch(ed, nc, "lnD", D)
                d["pj_ps"] = [ed.enter_context(nc.psum_tensor(f"pj_ps{i}", [128, 512], F32)) for i in range(4)]
                em.phase()
                c0 = em.barrier_counts()
                em.wait_all(cnts_B)
                emit_D(em, d)
                em.check_deadlock(c0)
                with nc.Block("phD") as blk:
                    em.replay(blk)

        em.phase()
        for s in ["LQ", "PQ", "GQ"]:
            em.wait("sync", s, em.cnt[s])
        with nc.Block("join") as blk:
            em.replay(blk)

    return nc


_NC_CACHE = {}


def _get_nc():
    if "nc" not in _NC_CACHE:
        _NC_CACHE["nc"] = build()
    return _NC_CACHE["nc"]


def _emb_tables():
    dim = HD
    theta = 10000.0 ** (-np.arange(0, dim, 2, dtype=np.float64) / dim)
    pos = np.arange(L, dtype=np.float64)
    freqs = pos[:, None] * theta[None, :]
    emb = np.concatenate([np.sin(freqs), np.cos(freqs)], axis=-1).astype(np.float32)
    tiled = emb.reshape(NT, 128, HD).transpose(1, 0, 2).reshape(128, NT * HD)
    return np.ascontiguousarray(tiled)


def make_inputs(L0_attn_x, L0_attn_Wqkv, L0_attn_bqkv, L0_attn_Wout, L0_attn_bout,
                L0_ffn_fast_b, L0_ffn_w_fc, L0_ffn_w_proj, L0_ffn_u, L0_ffn_v,
                L0_ffn_gate):
    x = np.ascontiguousarray(np.asarray(L0_attn_x, dtype=np.float32))
    embk = _emb_tables()
    embq = np.ascontiguousarray(embk / 8.0)
    common = {
        "wqkv": np.ascontiguousarray(np.asarray(L0_attn_Wqkv, np.float32)),
        "bqkv_r": np.ascontiguousarray(np.broadcast_to(
            np.asarray(L0_attn_bqkv, np.float32), (128, 3 * D))),
        "wout": np.ascontiguousarray(np.asarray(L0_attn_Wout, np.float32)),
        "bout_r": np.ascontiguousarray(np.broadcast_to(
            np.asarray(L0_attn_bout, np.float32), (128, D))),
        "wfc": np.ascontiguousarray(np.asarray(L0_ffn_w_fc, np.float32)),
        "wpj": np.ascontiguousarray(np.asarray(L0_ffn_w_proj, np.float32)),
        "uT": np.ascontiguousarray(np.asarray(L0_ffn_u, np.float32).T),
        "vTt": np.ascontiguousarray(
            np.asarray(L0_ffn_v, np.float32).T.reshape(8, 128, R)
            .transpose(1, 0, 2).reshape(128, 128)),
        "fb": np.ascontiguousarray(np.asarray(L0_ffn_fast_b, np.float32)),
        "gate_r": np.ascontiguousarray(np.broadcast_to(
            np.asarray(L0_ffn_gate, np.float32).reshape(1, 1), (R, 1))),
        "embq": embq,
        "embk": embk,
        "ident": np.eye(128, dtype=np.float32),
    }
    return [dict(common, x=np.ascontiguousarray(x[i])) for i in range(8)]


def kernel(L0_attn_x, L0_attn_Wqkv, L0_attn_bqkv, L0_attn_Wout, L0_attn_bout,
           L0_ffn_fast_b, L0_ffn_w_fc, L0_ffn_w_proj, L0_ffn_u, L0_ffn_v,
           L0_ffn_gate):
    in_maps = make_inputs(L0_attn_x, L0_attn_Wqkv, L0_attn_bqkv, L0_attn_Wout,
                          L0_attn_bout, L0_ffn_fast_b, L0_ffn_w_fc, L0_ffn_w_proj,
                          L0_ffn_u, L0_ffn_v, L0_ffn_gate)
    nc = _get_nc()
    res = run_bass_kernel_spmd(nc, in_maps, core_ids=list(range(8)))
    outs = [np.asarray(res.results[i]["out"], dtype=np.float32) for i in range(8)]
    return np.stack(outs, axis=0)


# version 15
# speedup vs baseline: 1.0003x; 1.0003x over previous
import math
import numpy as np
import ml_dtypes
from contextlib import ExitStack

import concourse.bass as bass
import concourse.mybir as mybir
from concourse.bass_utils import run_bass_kernel_spmd

F32 = mybir.dt.float32
BF16 = mybir.dt.bfloat16
AF = mybir.ActivationFunctionType
ALU = mybir.AluOpType
AX = mybir.AxisListType

L, D, H, HD, FF, R = 4096, 1024, 16, 64, 4096, 16
NT = L // 128
QB, NQB, NP = 512, 8, 8
EPS = 1e-5
SHIFT = 43.0
VW = D

ENGS = ("tensor", "vector", "scalar", "gpsimd", "sync")


class Em:

    def __init__(self, nc):
        self.nc = nc
        self.sems = {}
        self.cnt = {}
        self.q = None
        self.wm = None

    def add_sem(self, name, handle):
        self.sems[name] = handle
        self.cnt[name] = 0

    def phase(self):
        self.q = {e: [] for e in ENGS}
        self.wm = {}
        self.meta = {e: [] for e in ENGS}

    def wait(self, eng, sem, val):
        if val is None or val <= 0:
            return
        key = (eng, sem)
        if self.wm.get(key, -1) >= val:
            return
        self.wm[key] = val
        h = self.sems[sem]
        self.meta[eng].append(("wait", sem, val))
        self.q[eng].append(lambda E: E.wait_ge(h, val))

    def op(self, eng, fn, sig=None, n=1):
        if sig is None:
            self.meta[eng].append(("op", None, 0))
            self.q[eng].append(lambda E: fn(E))
            return None
        self.cnt[sig] += n
        h = self.sems[sig]
        self.meta[eng].append(("op", sig, n))
        self.q[eng].append(lambda E: fn(E).then_inc(h, n))
        return self.cnt[sig]

    def check_deadlock(self, start_counts):
        cnt = dict(start_counts)
        pos = {e: 0 for e in ENGS}
        progress = True
        while progress:
            progress = False
            for e in ENGS:
                while pos[e] < len(self.meta[e]):
                    kind, sem, val = self.meta[e][pos[e]]
                    if kind == "wait" and cnt.get(sem, 0) < val:
                        break
                    if kind == "op" and sem is not None:
                        cnt[sem] = cnt.get(sem, 0) + val
                    pos[e] += 1
                    progress = True
        stuck = {e: (pos[e], len(self.meta[e]), self.meta[e][pos[e]]
                     if pos[e] < len(self.meta[e]) else None)
                 for e in ENGS if pos[e] < len(self.meta[e])}
        if stuck:
            raise AssertionError(f"emission deadlock: {stuck}; counts={ {k: v for k, v in cnt.items()} }")

    def dma(self, eng, sig, out, in_, **kw):
        return self.op(eng, lambda E: E.dma_start(out, in_, **kw), sig, 16)

    def tread(self, eng, sig, out, in_):
        return self.op(eng, lambda E: E.dma_start(out, in_, transpose=True), sig, 16)

    def barrier_counts(self):
        return dict(self.cnt)

    def wait_all(self, counts):
        for e in ENGS:
            for s, v in counts.items():
                self.wait(e, s, v)

    def replay(self, block):
        qs = self.q

        @block.tensor
        def _(E):
            for f in qs["tensor"]:
                f(E)

        @block.vector
        def _(E):
            for f in qs["vector"]:
                f(E)

        @block.scalar
        def _(E):
            for f in qs["scalar"]:
                f(E)

        @block.gpsimd
        def _(E):
            for f in qs["gpsimd"]:
                f(E)

        @block.sync
        def _(E):
            for f in qs["sync"]:
                f(E)


def emit_ln(em, x_ap, n_free, scr, out_ap):
    fs = scr["fs"]

    def sfence():
        return em.op("scalar", lambda E: E.activation(
            fs[0:1, 2:3], fs[0:1, 3:4], AF.Square, bias=0.0, scale=1.0), "SS")

    v1 = em.op("vector", lambda E: E.tensor_reduce(
        scr["sumx"][:, :], x_ap, AX.X, ALU.add), "VS")
    ss_sq = em.op("scalar", lambda E: E.activation(
        scr["sq"][:, 0:n_free], x_ap, AF.Square, bias=0.0, scale=1.0,
        accum_out=scr["ssq"][:, :]), "SS")
    em.wait("scalar", "SS", ss_sq)
    ssf = sfence()
    em.wait("vector", "SS", ssf)
    em.wait("vector", "VS", v1)
    v2 = em.op("vector", lambda E: E.tensor_scalar(
        scr["mean"][:, :], scr["sumx"][:, :], 1.0 / n_free, None, ALU.mult), "VS")
    em.wait("vector", "VS", v2)
    v3 = em.op("vector", lambda E: E.tensor_tensor(
        scr["m2"][:, :], scr["mean"][:, :], scr["mean"][:, :], ALU.mult), "VS")
    em.wait("vector", "VS", v3)
    v4 = em.op("vector", lambda E: E.scalar_tensor_tensor(
        scr["var"][:, :], scr["ssq"][:, :], 1.0 / n_free, scr["m2"][:, :],
        ALU.mult, ALU.subtract), "VS")
    em.wait("vector", "VS", v4)
    v5 = em.op("vector", lambda E: E.tensor_scalar(
        scr["var"][:, :], scr["var"][:, :], EPS, None, ALU.add), "VS")
    em.wait("vector", "VS", v5)
    vs_r = em.op("vector", lambda E: E.reciprocal(
        scr["rcp"][:, :], scr["var"][:, :]), "VS")
    em.wait("vector", "VS", vs_r)
    vs_rf = em.op("vector", lambda E: E.tensor_copy(
        fs[0:1, 0:1], fs[0:1, 1:2]), "VS")
    em.wait("scalar", "VS", vs_rf)
    s1 = em.op("scalar", lambda E: E.activation(
        scr["lnt"][:, :], scr["rcp"][:, :], AF.Ln, bias=0.0, scale=1.0), "SS")
    em.wait("scalar", "SS", s1)
    ss_rs = em.op("scalar", lambda E: E.activation(
        scr["rs"][:, :], scr["lnt"][:, :], AF.Exp, bias=0.0, scale=0.5), "SS")
    em.wait("scalar", "SS", ss_rs)
    ssf = sfence()
    em.wait("vector", "SS", ssf)
    v6 = em.op("vector", lambda E: E.scalar_tensor_tensor(
        scr["nmrs"][:, :], scr["mean"][:, :], -1.0, scr["rs"][:, :],
        ALU.mult, ALU.mult), "VS")
    em.wait("vector", "VS", v6)
    vs_ap = em.op("vector", lambda E: E.tensor_scalar(
        out_ap, x_ap, scr["rs"][:, :], scr["nmrs"][:, :], ALU.mult, ALU.add), "VS")
    return vs_ap, ss_sq


def ln_stat(em, x_ap, n_free, scr):
    em.op("vector", lambda E: E.tensor_reduce(scr["sumx"][:, :], x_ap, AX.X, ALU.add))
    em.op("scalar", lambda E: E.activation(
        scr["sq"][:, 0:n_free], x_ap, AF.Square, bias=0.0, scale=1.0,
        accum_out=scr["ssq"][:, :]), "SS")
    fs = scr["fs"]
    return em.op("scalar", lambda E: E.activation(
        fs[0:1, 2:3], fs[0:1, 3:4], AF.Square, bias=0.0, scale=1.0), "SS")


def ln_fin(em, n_free, scr, ss_tok):
    fs = scr["fs"]
    em.wait("vector", "SS", ss_tok)
    v2 = em.op("vector", lambda E: E.tensor_scalar(
        scr["mean"][:, :], scr["sumx"][:, :], 1.0 / n_free, None, ALU.mult), "VS")
    em.wait("vector", "VS", v2)
    v3 = em.op("vector", lambda E: E.tensor_tensor(
        scr["m2"][:, :], scr["mean"][:, :], scr["mean"][:, :], ALU.mult), "VS")
    em.wait("vector", "VS", v3)
    v4 = em.op("vector", lambda E: E.scalar_tensor_tensor(
        scr["var"][:, :], scr["ssq"][:, :], 1.0 / n_free, scr["m2"][:, :],
        ALU.mult, ALU.subtract), "VS")
    em.wait("vector", "VS", v4)
    v5 = em.op("vector", lambda E: E.tensor_scalar(
        scr["var"][:, :], scr["var"][:, :], EPS, None, ALU.add), "VS")
    em.wait("vector", "VS", v5)
    vs_r = em.op("vector", lambda E: E.reciprocal(
        scr["rcp"][:, :], scr["var"][:, :]), "VS")
    em.wait("vector", "VS", vs_r)
    vs_rf = em.op("vector", lambda E: E.tensor_copy(
        fs[0:1, 0:1], fs[0:1, 1:2]), "VS")
    em.wait("scalar", "VS", vs_rf)
    s1 = em.op("scalar", lambda E: E.activation(
        scr["lnt"][:, :], scr["rcp"][:, :], AF.Ln, bias=0.0, scale=1.0), "SS")
    em.wait("scalar", "SS", s1)
    em.op("scalar", lambda E: E.activation(
        scr["rs"][:, :], scr["lnt"][:, :], AF.Exp, bias=0.0, scale=0.5), "SS")
    return em.op("scalar", lambda E: E.activation(
        fs[0:1, 2:3], fs[0:1, 3:4], AF.Square, bias=0.0, scale=1.0), "SS")


def ln_apply(em, x_ap, scr, ss_tok, out_ap):
    em.wait("vector", "SS", ss_tok)
    v6 = em.op("vector", lambda E: E.scalar_tensor_tensor(
        scr["nmrs"][:, :], scr["mean"][:, :], -1.0, scr["rs"][:, :],
        ALU.mult, ALU.mult), "VS")
    em.wait("vector", "VS", v6)
    return em.op("vector", lambda E: E.tensor_scalar(
        out_ap, x_ap, scr["rs"][:, :], scr["nmrs"][:, :], ALU.mult, ALU.add), "VS")


def ln_scratch(es, nc, tag, n_free):
    scr = {"sq": es.enter_context(nc.sbuf_tensor(f"sq_{tag}", [128, n_free], BF16))}
    for k in ["ssq", "mean", "m2", "var", "rcp", "lnt", "rs", "nmrs", "sumx"]:
        scr[k] = es.enter_context(nc.sbuf_tensor(f"{tag}_{k}", [128, 1], F32))
    scr["fs"] = es.enter_context(nc.sbuf_tensor(f"{tag}_fs", [1, 4], F32))
    return scr


def emit_A(em, g):
    (wq_sb, embq_sb, embk_sb, bqkv_sb, xt, x1b, x1t, qkvb, rtmp, qo, ko, vo,
     lnA, hn, qn, kn) = (g[k] for k in [
        "wq_sb", "embq_sb", "embk_sb", "bqkv_sb", "xt", "x1b", "x1t", "qkvb",
        "rtmp", "qo", "ko", "vo", "lnA", "hn", "qn", "kn"])
    (nbtb, qm2, km2, Bqk, Bqkb, Btq, pBn, nb164) = (g[k] for k in [
        "nbtb", "qm2", "km2", "Bqk", "Bqkb", "Btq", "pBn", "nb164"])
    (wp_in, wp_t, wp_g, wp_o, wp_s) = (g[k] for k in [
        "wp_in", "wp_t", "wp_g", "wp_o", "wp_s"])
    (uT_sb, fb_sb, gate_sb, fbg, fbgb) = (g[k] for k in [
        "uT_sb", "fb_sb", "gate_sb", "fbg", "fbgb"])
    qkv_ps, tp_ps, ms_ps = g["qkv_ps"], g["tp_ps"], g["ms_ps"]
    (ident, ones32, negB, bout_sb, gW2, vT) = (g[k] for k in [
        "ident", "ones32", "negB", "bout_sb", "gW2", "vT"])
    (x_in, wqkv, bqkv_r, bout_r, wfc_in, wpj_in, uT_in, vT_in,
     fb_in, gate_r, embq_in, embk_in, ident_in) = (g[k] for k in [
        "x_in", "wqkv", "bqkv_r", "bout_r", "wfc_in", "wpj_in",
        "uT_in", "vT_in", "fb_in", "gate_r", "embq_in", "embk_in", "ident_in"])
    (s_q, s_k, s_v, s_fceff, s_fcT, s_pjeff, s_nb, s_x1) = (g[k] for k in [
        "s_q", "s_k", "s_v", "s_fceff", "s_fcT", "s_pjeff", "s_nb", "s_x1"])
    sq2 = lnA["sq"]

    wq3 = wqkv.rearrange("(c p) n -> p c n", p=128)
    wq_dst = wq_sb[:, :].rearrange("p (c n) -> p c n", c=8)
    gq_wq = 0
    for h in range(2):
        gq_wq = em.dma("gpsimd", "GQ", wq_dst[:, 4 * h:4 * h + 4, :],
                       wq3[:, 4 * h:4 * h + 4, :])
    gq_id = em.dma("gpsimd", "GQ", ident[:, :], ident_in)
    gq_uT = em.dma("gpsimd", "GQ", uT_sb[0:R, :], uT_in)
    gq_vT = em.dma("gpsimd", "GQ", vT[:, :], vT_in)
    em.op("gpsimd", lambda E: E.memset(ones32[:, :], 1.0))
    em.op("gpsimd", lambda E: E.memset(qn[:, :], 0.0))
    gs_init = em.op("gpsimd", lambda E: E.memset(kn[:, :], 0.0), "GS")

    lq_embq = em.dma("sync", "LQ", embq_sb[:, :], embq_in)
    lq_embk = em.dma("sync", "LQ", embk_sb[:, :], embk_in)
    lq_bq = em.dma("sync", "LQ", bqkv_sb[:, :], bqkv_r)
    em.dma("sync", "LQ", bout_sb[:, :], bout_r)
    em.dma("sync", "LQ", fb_sb[0:R, :], fb_in)
    lq_gate = em.dma("sync", "LQ", gate_sb[0:R, :], gate_r)
    x_tok = x_in.rearrange("(t p) d -> t p d", p=128)
    lq_x = {i: em.dma("sync", "LQ", xt[i][:, :], x_tok[i]) for i in range(2)}

    wfc_t = wfc_in.rearrange("(t p) d -> t p d", p=128)
    NU = 48

    def prep_src(j):
        if j < 32:
            return wfc_t[j]
        tile, s = (j - 32) // 2, (j - 32) % 2
        return wpj_in[128 * tile:128 * (tile + 1), 2048 * s:2048 * (s + 1)]

    def prep_w(j):
        return D if j < 32 else 2048

    pq_wp = {0: em.dma("scalar", "PQ", wp_in[0][:, 0:D], prep_src(0)),
             1: em.dma("scalar", "PQ", wp_in[1][:, 0:D], prep_src(1))}

    prep_state = {"done": 0, "fc_store": 0, "pj_store": 0}

    def prep_unit(j):
        sl = j % 2
        w = prep_w(j)
        src = wp_in[sl]
        em.wait("scalar", "PQ", pq_wp[j])
        is_pj = j >= 32
        half2 = is_pj and ((j - 32) % 2 == 1)
        scol = 1 if half2 else 0
        ss_a = em.op("scalar", lambda E, w=w, scol=scol, sl=sl: E.activation(
            wp_t[:, 0:w], wp_in[sl][:, 0:w], AF.Abs, bias=0.0, scale=1.0,
            accum_out=wp_s[:, scol:scol + 1]), "SS")
        em.wait("scalar", "SS", ss_a)
        ss_a = em.op("scalar", lambda E: E.activation(
            g["fscB"][0:1, 2:3], g["fscB"][0:1, 3:4], AF.Square,
            bias=0.0, scale=1.0), "SS")
        def emit_scale_and_tern(jj, sl2, w2):
            denom = D if jj < 32 else FF
            if jj < 32:
                vs_sc = em.op("vector", lambda E: E.tensor_scalar(
                    wp_s[:, 2:3], wp_s[:, 0:1], 1.0 / denom, 1e-5, ALU.mult, ALU.max), "VS")
            else:
                v_add = em.op("vector", lambda E: E.tensor_tensor(
                    wp_s[:, 2:3], wp_s[:, 0:1], wp_s[:, 1:2], ALU.add), "VS")
                em.wait("vector", "VS", v_add)
                vs_sc = em.op("vector", lambda E: E.tensor_scalar(
                    wp_s[:, 2:3], wp_s[:, 2:3], 1.0 / denom, 1e-5, ALU.mult, ALU.max), "VS")
            em.wait("vector", "VS", vs_sc)
            vs_rp = em.op("vector", lambda E: E.reciprocal(wp_s[:, 3:4], wp_s[:, 2:3]), "VS")
            em.wait("vector", "VS", vs_rp)
            v_ng = em.op("vector", lambda E: E.tensor_scalar(
                wp_s[:, 2:3], wp_s[:, 2:3], -1.0, None, ALU.mult), "VS")
            em.wait("vector", "VS", v_ng)
            vs_t = em.op("vector", lambda E, sl2=sl2, w2=w2: E.tensor_scalar(
                wp_t[:, 0:w2], wp_in[sl2][:, 0:w2], wp_s[:, 3:4], None, ALU.mult), "VS")
            em.op("vector", lambda E, w2=w2: E.tensor_scalar(
                wp_g[:, 0:w2], wp_t[:, 0:w2], 0.5, None, ALU.is_ge))
            em.op("vector", lambda E, w2=w2: E.scalar_tensor_tensor(
                wp_g[:, 0:w2], wp_t[:, 0:w2], -0.5, wp_g[:, 0:w2],
                ALU.is_le, ALU.subtract))
            vs_o = em.op("vector", lambda E, sl2=sl2, w2=w2: E.tensor_scalar(
                wp_o[sl2][:, 0:w2], wp_g[:, 0:w2], wp_s[:, 2:3], None, ALU.mult), "VS")
            em.wait("scalar", "VS", vs_o)
            return vs_t

        if not is_pj:
            em.wait("vector", "SS", ss_a)
            vs_t = emit_scale_and_tern(j, sl, D)
            dst = s_fceff.rearrange("(t p) d -> t p d", p=128)[j]
            prep_state["fc_store"] = em.dma("scalar", "PQ", dst, wp_o[sl][:, 0:D])
            jn = j + 2
            if jn < NU:
                em.wait("scalar", "VS", vs_t)
                pq_wp[jn] = em.dma("scalar", "PQ", wp_in[sl][:, 0:prep_w(jn)],
                                   prep_src(jn))
        elif half2:
            em.wait("vector", "SS", ss_a)
            tile = (j - 32) // 2
            for hh in range(2):
                sl2 = (j - 1 + hh) % 2
                vs_t = emit_scale_and_tern(j, sl2, 2048)
                prep_state["pj_store"] = em.dma(
                    "scalar", "PQ",
                    s_pjeff[128 * tile:128 * (tile + 1), 2048 * hh:2048 * (hh + 1)],
                    wp_o[sl2][:, 0:2048])
                jn = j - 1 + hh + 2
                if jn < NU:
                    em.wait("scalar", "VS", vs_t)
                    pq_wp[jn] = em.dma("scalar", "PQ",
                                       wp_in[sl2][:, 0:prep_w(jn)], prep_src(jn))

    q3b = qkvb[:, 0:D].rearrange("p (h d) -> p h d", h=16)
    k3b = qkvb[:, D:2 * D].rearrange("p (h d) -> p h d", h=16)
    r3 = rtmp[:, :].rearrange("p (h d) -> p h d", h=16)

    last_evac = {ch: 0 for ch in range(6)}
    last_x1t_use = {0: 0, 1: 0}
    vs_norm_last = 0
    st_tok = {"q": {}, "k": {}, "v": {}}
    x1_sp = {}
    x1t_ld = {}

    def emit_lnx(i):
        sx = i % 2
        sl = i % 2
        em.wait("vector", "LQ", lq_x[i])
        em.wait("scalar", "LQ", lq_x[i])
        em.wait("scalar", "VS", vs_norm_last)
        em.wait("vector", "PQ", x1_sp.get(i - 2, 0))
        vs_x1, ss_x1 = emit_ln(em, xt[sx][:, :], D, lnA, x1b[sl][:, :])
        if i + 2 < NT:
            em.wait("sync", "VS", vs_x1)
            em.wait("sync", "SS", ss_x1)
            lq_x[i + 2] = em.dma("sync", "LQ", xt[(i + 2) % 2][:, :], x_tok[i + 2])
        em.wait("scalar", "VS", vs_x1)
        x1_sp[i] = em.dma("scalar", "PQ",
                          s_x1.rearrange("(t p) d -> t p d", p=128)[i], x1b[sl][:, :])

    def emit_x1t(gr):
        em.wait("sync", "PQ", x1_sp[4 * gr + 3])
        em.wait("sync", "TS", last_x1t_use[gr % 2])
        lq = 0
        for c in range(8):
            lq = em.tread("sync", "LQ", x1t[gr % 2][:, 512 * c:512 * (c + 1)],
                          s_x1[512 * gr:512 * (gr + 1), 128 * c:128 * (c + 1)])
        x1t_ld[gr] = lq

    def emit_qkv(i):
        gr = i // 4
        m = i % 4
        sl = i % 2
        em.wait("tensor", "LQ", x1t_ld[gr])
        em.wait("tensor", "GQ", gq_wq)
        ts_chunk = {}
        for ch in range(6):
            em.wait("tensor", "VS", last_evac[ch])
            for c in range(8):
                tok = em.op("tensor", lambda E, ch=ch, c=c, gr=gr, m=m: E.matmul(
                    qkv_ps[:, 512 * ch:512 * (ch + 1)],
                    x1t[gr % 2][:, 512 * c + 128 * m: 512 * c + 128 * (m + 1)],
                    wq_sb[:, 3072 * c + 512 * ch: 3072 * c + 512 * (ch + 1)],
                    start=(c == 0), stop=(c == 7)), "TS" if c == 7 else None)
            ts_chunk[ch] = tok
        last_x1t_use[gr % 2] = em.cnt["TS"]
        em.wait("vector", "LQ", lq_bq)
        for ch in range(6):
            em.wait("vector", "TS", ts_chunk[ch])
            last_evac[ch] = em.op("vector", lambda E, ch=ch: E.tensor_tensor(
                qkvb[:, 512 * ch:512 * (ch + 1)], qkv_ps[:, 512 * ch:512 * (ch + 1)],
                bqkv_sb[:, 512 * ch:512 * (ch + 1)], ALU.add), "VS")
        if i >= 2:
            em.wait("vector", "LQ", st_tok["v"][i - 2])
        em.wait("vector", "LQ", lq_embq)
        em.wait("vector", "LQ", lq_embk)
        em.wait("vector", "VS", last_evac[1])
        eq3 = embq_sb[:, HD * i:HD * (i + 1)].rearrange("p (o d) -> p o d", o=1) \
            .broadcast_to([128, 16, HD])
        em.op("vector", lambda E: E.tensor_tensor(
            r3[:, :, 0:32], q3b[:, :, 0:32], q3b[:, :, 1:64:2], ALU.subtract))
        v_r = em.op("vector", lambda E: E.tensor_tensor(
            r3[:, :, 32:64], q3b[:, :, 32:64], q3b[:, :, 0:64:2], ALU.add), "VS")
        em.wait("vector", "VS", v_r)
        vs_q = em.op("vector", lambda E, sl=sl, eq3=eq3: E.tensor_tensor(
            qo[sl][:, :].rearrange("p (h d) -> p h d", h=16), r3, eq3, ALU.mult), "VS")
        em.wait("vector", "VS", vs_q)
        em.wait("vector", "VS", last_evac[3])
        ek3 = embk_sb[:, HD * i:HD * (i + 1)].rearrange("p (o d) -> p o d", o=1) \
            .broadcast_to([128, 16, HD])
        em.op("vector", lambda E: E.tensor_tensor(
            r3[:, :, 0:32], k3b[:, :, 0:32], k3b[:, :, 1:64:2], ALU.subtract))
        v_r = em.op("vector", lambda E: E.tensor_tensor(
            r3[:, :, 32:64], k3b[:, :, 32:64], k3b[:, :, 0:64:2], ALU.add), "VS")
        em.wait("vector", "VS", v_r)
        vs_k = em.op("vector", lambda E, sl=sl, ek3=ek3: E.tensor_tensor(
            ko[sl][:, :].rearrange("p (h d) -> p h d", h=16), r3, ek3, ALU.mult), "VS")
        em.wait("vector", "VS", last_evac[5])
        vs_v = em.op("vector", lambda E, sl=sl: E.tensor_copy(
            vo[sl][:, :], qkvb[:, 2 * D:3 * D]), "VS")
        em.wait("scalar", "VS", vs_q)
        ss_qs = em.op("scalar", lambda E, sl=sl: E.activation(
            sq2[:, :], qo[sl][:, :], AF.Square, bias=0.0, scale=1.0), "SS")
        em.wait("vector", "SS", ss_qs)
        em.wait("vector", "GS", gs_init)
        v_h = em.op("vector", lambda E: E.tensor_reduce(
            hn[:, :], sq2[:, :].rearrange("p (h d) -> p h d", h=16), AX.X, ALU.add), "VS")
        em.wait("vector", "VS", v_h)
        qb_i = i // 4
        vs_qn = em.op("vector", lambda E, qb_i=qb_i: E.tensor_tensor(
            qn[:, 16 * qb_i:16 * (qb_i + 1)], qn[:, 16 * qb_i:16 * (qb_i + 1)],
            hn[:, :], ALU.max), "VS")
        em.wait("scalar", "VS", vs_qn)
        ss_ks = em.op("scalar", lambda E, sl=sl: E.activation(
            sq2[:, :], ko[sl][:, :], AF.Square, bias=0.0, scale=1.0), "SS")
        em.wait("vector", "SS", ss_ks)
        em.wait("vector", "VS", vs_qn)
        v_h = em.op("vector", lambda E: E.tensor_reduce(
            hn[:, :], sq2[:, :].rearrange("p (h d) -> p h d", h=16), AX.X, ALU.add), "VS")
        em.wait("vector", "VS", v_h)
        nonlocal_vs = em.op("vector", lambda E: E.tensor_tensor(
            kn[:, :], kn[:, :], hn[:, :], ALU.max), "VS")
        em.wait("sync", "VS", vs_q)
        st_tok["q"][i] = em.dma("sync", "LQ",
                                s_q.rearrange("(t p) d -> t p d", p=128)[i], qo[sl][:, :])
        em.wait("sync", "VS", vs_k)
        st_tok["k"][i] = em.dma("sync", "LQ",
                                s_k.rearrange("(t p) d -> t p d", p=128)[i], ko[sl][:, :])
        em.wait("sync", "VS", vs_v)
        st_tok["v"][i] = em.dma("sync", "LQ",
                                s_v.rearrange("(t p) d -> t p d", p=128)[i], vo[sl][:, :])
        return nonlocal_vs

    for gr in range(9):
        if gr < 8:
            for i in range(4 * gr, 4 * gr + 4):
                emit_lnx(i)
            emit_x1t(gr)
        if gr >= 1:
            for i in range(4 * (gr - 1), 4 * (gr - 1) + 4):
                vs_norm_last = emit_qkv(i)
                if prep_state["done"] < NU:
                    budget = 2 if prep_state["done"] < 32 else 1
                    for _ in range(budget):
                        if prep_state["done"] < NU:
                            prep_unit(prep_state["done"])
                            prep_state["done"] += 1
    while prep_state["done"] < NU:
        prep_unit(prep_state["done"])
        prep_state["done"] += 1

    fctb = wp_in[0][:, :].bitcast(BF16)
    pq_fc_wr = {}
    em.wait("sync", "VS", em.cnt["VS"])
    for c in range(8):
        em.wait("sync", "PQ", prep_state["fc_store"])
        if c >= 1:
            em.wait("sync", "PQ", pq_fc_wr[c - 1])
        lq_t = em.tread("sync", "LQ", fctb, s_fceff[0:FF, 128 * c:128 * (c + 1)])
        em.wait("scalar", "LQ", lq_t)
        pq_fc_wr[c] = em.dma("scalar", "PQ",
                             s_fcT.rearrange("(c p) n -> c p n", p=128)[c], fctb)

    fsc = g["fscB"]

    def vfence():
        return em.op("vector", lambda E: E.tensor_copy(
            fsc[0:1, 0:1], fsc[0:1, 1:2]), "VS")

    for qb in range(NQB):
        vs_c = em.op("vector", lambda E, qb=qb: E.tensor_copy(
            nbtb[:, :], qn[:, 16 * qb:16 * (qb + 1)]), "VS")
        em.wait("vector", "VS", vs_c)
        vs_c = vfence()
        em.wait("tensor", "VS", vs_c)
        ts_t = em.op("tensor", lambda E: E.transpose(
            tp_ps[0:16, 0:128], nbtb[:, :], ident[:, :]), "TS")
        em.wait("vector", "TS", ts_t)
        vs_m = em.op("vector", lambda E, qb=qb: E.tensor_reduce(
            qm2[0:16, qb:qb + 1], tp_ps[0:16, 0:128], AX.X, ALU.max), "VS")
        em.wait("tensor", "VS", vs_m)
    vs_c = em.op("vector", lambda E: E.tensor_copy(nbtb[:, :], kn[:, :]), "VS")
    em.wait("vector", "VS", vs_c)
    vs_c = vfence()
    em.wait("tensor", "VS", vs_c)
    ts_t = em.op("tensor", lambda E: E.transpose(
        tp_ps[0:16, 0:128], nbtb[:, :], ident[:, :]), "TS")
    em.wait("vector", "TS", ts_t)
    vs_km = em.op("vector", lambda E: E.tensor_reduce(
        km2[0:16, 0:1], tp_ps[0:16, 0:128], AX.X, ALU.max), "VS")
    em.wait("scalar", "VS", vs_km)
    s_q1 = em.op("scalar", lambda E: E.activation(
        qm2[0:16, :], qm2[0:16, :], AF.Ln, bias=0.0, scale=1.0), "SS")
    em.wait("scalar", "SS", s_q1)
    ss_l = em.op("scalar", lambda E: E.activation(
        km2[0:16, :], km2[0:16, :], AF.Ln, bias=0.0, scale=1.0), "SS")
    em.wait("scalar", "SS", ss_l)
    ss_l = em.op("scalar", lambda E: E.activation(
        fsc[0:1, 2:3], fsc[0:1, 3:4], AF.Square, bias=0.0, scale=1.0), "SS")
    em.wait("vector", "SS", ss_l)
    vs_s = em.op("vector", lambda E: E.tensor_scalar(
        Bqk[0:16, :], qm2[0:16, :], km2[0:16, 0:1], None, ALU.add), "VS")
    em.wait("vector", "VS", vs_s)
    em.wait("scalar", "VS", vs_s)
    ss_e = em.op("scalar", lambda E: E.activation(
        Bqk[0:16, :], Bqk[0:16, :], AF.Exp, bias=0.0, scale=0.5), "SS")
    em.wait("scalar", "SS", ss_e)
    ss_e = em.op("scalar", lambda E: E.activation(
        fsc[0:1, 2:3], fsc[0:1, 3:4], AF.Square, bias=0.0, scale=1.0), "SS")
    em.wait("vector", "SS", ss_e)
    vs_b = em.op("vector", lambda E: E.tensor_copy(Bqkb[0:16, :], Bqk[0:16, :]), "VS")
    em.wait("vector", "VS", vs_b)
    vs_b = vfence()
    em.wait("tensor", "VS", vs_b)
    ts_t2 = em.op("tensor", lambda E: E.transpose(
        tp_ps[0:8, 0:16], Bqkb[0:16, 0:NQB], ident[0:16, 0:16]), "TS")
    em.wait("vector", "TS", ts_t2)
    v_c = em.op("vector", lambda E: E.tensor_copy(Btq[0:8, :], tp_ps[0:8, 0:16]), "VS")
    em.wait("vector", "VS", v_c)
    v_p = em.op("vector", lambda E: E.tensor_tensor(
        pBn[0:8, :], Btq[0:8, 0:16:2], Btq[0:8, 1:16:2], ALU.max), "VS")
    em.wait("vector", "VS", v_p)
    vs_nb = em.op("vector", lambda E: E.tensor_scalar(
        pBn[0:8, :], pBn[0:8, :], -1.0, SHIFT, ALU.mult, ALU.add), "VS")
    em.wait("scalar", "VS", vs_nb)
    pq_nb = em.dma("scalar", "PQ", s_nb, pBn[0:8, 0:NP])
    em.wait("sync", "PQ", pq_nb)
    lq_nb = em.dma("sync", "LQ", nb164[0:1, :],
                   s_nb.rearrange("a b -> (a b)").rearrange("(o n) -> o n", o=1))
    em.wait("tensor", "LQ", lq_nb)
    em.wait("tensor", "GS", gs_init)
    ts_r = em.op("tensor", lambda E: E.matmul(
        ms_ps[:, 0:NQB * NP], ones32[0:1, 0:128], nb164[0:1, :],
        start=True, stop=True), "TS")
    em.wait("vector", "TS", ts_r)
    vs_nbr = em.op("vector", lambda E: E.tensor_copy(
        negB[:, :], ms_ps[:, 0:NQB * NP]), "VS")

    em.wait("vector", "LQ", lq_gate)
    v_fg = em.op("vector", lambda E: E.tensor_scalar(
        fbg[0:16, :], fb_sb[0:16, :], gate_sb[0:16, 0:1], None, ALU.mult), "VS")
    em.wait("vector", "VS", v_fg)
    vs_f = em.op("vector", lambda E: E.tensor_copy(fbgb[0:16, :], fbg[0:16, :]), "VS")
    em.wait("vector", "VS", vs_f)
    vs_f = vfence()
    em.wait("tensor", "VS", vs_f)
    em.wait("tensor", "VS", vs_nbr)
    em.wait("tensor", "GQ", gq_uT)
    for n in range(8):
        ts_g = em.op("tensor", lambda E, n=n: E.matmul(
            ms_ps[0:16, 0:512], fbgb[0:16, 0:16], uT_sb[0:16, 512 * n:512 * (n + 1)],
            start=True, stop=True), "TS")
        em.wait("vector", "TS", ts_g)
        vs_g = em.op("vector", lambda E, n=n: E.tensor_copy(
            gW2[0:16, 512 * n:512 * (n + 1)], ms_ps[0:16, 0:512]), "VS")
        em.wait("tensor", "VS", vs_g)


def emit_B(em, g):
    (vpr, ktb, qtb, expb, attnT, rcpb, Ssb, fcT, xrt, Asb, ysb, x2b,
     x2t, t1b, t1T, rsb, rrb, lnB, wout_sb, onesb) = (g[k] for k in [
        "vpr", "ktb", "qtb", "expb", "attnT", "rcpb", "Ssb", "fcT", "xrt",
        "Asb", "ysb", "x2b", "x2t", "t1b", "t1T", "rsb", "rrb", "lnB",
        "wout_sb", "onesb"])
    sc_ps, pvp, sums_ps, msB_ps = (
        g["sc_ps"], g["pvp"], g["sums_ps"], g["msB_ps"])
    (ident, ones32, negB, bout_sb, gW2, vT) = (g[k] for k in [
        "ident", "ones32", "negB", "bout_sb", "gW2", "vT"])
    (x_in, wout_in, s_q, s_k, s_v, s_fcT, s_y, s_rr) = (g[k] for k in [
        "x_in", "wout_in", "s_q", "s_k", "s_v", "s_fcT", "s_y", "s_rr"])

    x_tok = x_in.rearrange("(t p) d -> t p d", p=128)
    y_tok = s_y.rearrange("(t p) d -> t p d", p=128)
    fcT_src = s_fcT.rearrange("(c p) f -> p c f", p=128)

    st = {
        "qt_free": {0: 0, 1: 0},
        "exp_free": {0: 0, 1: 0, 2: 0, 3: 0}, "last_exp": 0, "pv_free": 0,
        "ms": 0, "tp": 0,
        "fcT_free": {0: 0, 1: 0}, "x_free": {0: 0, 1: 0},
        "y_free": {0: 0, 1: 0}, "x2_free": 0, "rr_free": 0,
        "A_free": {0: 0, 1: 0}, "ysq_free": {0: 0, 1: 0},
        "rsb_free": {0: 0, 1: 0}, "slot_free": {},
        "x2t_free": 0, "s_last": 0,
        "kq_free": {0: 0, 1: 0, 2: 0},
        "last_rc": 0, "last_exp2": {0: 0, 1: 0},
    }
    kq_load, qt_load, fcT_load, x_load = {}, {}, {}, {}
    defer_q = []
    epi_q = []
    fsc = g["fscB"]

    def vfence():
        return em.op("vector", lambda E: E.tensor_copy(
            fsc[0:1, 0:1], fsc[0:1, 1:2]), "VS")
    misc_q = []

    def misc_pop(n=1):
        for _ in range(n):
            if misc_q:
                misc_q.pop(0)()

    gq_wo = em.dma("gpsimd", "GQ",
                   wout_sb[:, :].rearrange("p (c n) -> p c n", c=8),
                   wout_in.rearrange("(c p) n -> p c n", p=128))
    sv3 = s_v.rearrange("(t p) d -> p t d", p=128)
    vpr3 = vpr[:, :].rearrange("p (t d) -> p t d", t=NT)
    lq_v = 0
    for h in range(4):
        lq_v = em.dma("sync", "LQ", vpr3[:, 8 * h:8 * (h + 1), :],
                      sv3[:, 8 * h:8 * (h + 1), :])
    gs_ob = em.op("gpsimd", lambda E: E.memset(onesb[:, :], 1.0), "GS")

    def kq_slot(qb, p_i, qt):
        return ((qb * NP + p_i) * 4 + qt) % 3

    def kq_prefetch(qb, p_i, qt):
        if qb >= NQB or (qb, p_i, qt) in kq_load:
            return
        sslot = kq_slot(qb, p_i, qt)
        em.wait("sync", "TS", st["kq_free"][sslot])
        kq_load[(qb, p_i, qt)] = em.tread(
            "sync", "LQ", ktb[sslot][:, :],
            s_k[1024 * qt:1024 * (qt + 1), 128 * p_i:128 * (p_i + 1)])

    def qt_prefetch(qb, p_i):
        if qb >= NQB or (qb, p_i) in qt_load:
            return
        nsl = p_i % 2
        em.wait("sync", "TS", st["qt_free"][nsl])
        qt_load[(qb, p_i)] = em.tread(
            "sync", "LQ", qtb[nsl][:, :],
            s_q[qb * QB:(qb + 1) * QB, 128 * p_i:128 * (p_i + 1)])

    kq_prefetch(0, 0, 0)
    kq_prefetch(0, 0, 1)
    qt_prefetch(0, 0)

    def make_misc(qb):
        slot = lambda pp: (qb * 8 + pp) % 10
        loc = {"x2t": {}, "t1": {}}

        def g_fcload(n):
            def go():
                fsl = n % 2
                em.wait("sync", "TS", st["fcT_free"][fsl])
                fcT_load[(qb, n)] = em.dma(
                    "sync", "LQ", fcT[fsl][:, :], fcT_src[:, :, 512 * n:512 * (n + 1)])
            return go

        def g_xload(m):
            def go():
                em.wait("sync", "VS", st["x_free"][m % 2])
                x_load[(qb, m)] = em.dma("sync", "LQ", xrt[m % 2][:, :],
                                         x_tok[qb * 4 + m])
            return go

        def g_w(m, n):
            def go():
                em.wait("tensor", "VS", st["pv_free"])
                if n == 0:
                    em.wait("vector", "SS", st["A_free"][m % 2])
                em.wait("tensor", "VS", st["ms"])
                em.wait("tensor", "GQ", gq_wo)
                ts_w = None
                for c in range(8):
                    ts_w = em.op("tensor", lambda E, c=c, m=m, n=n: E.matmul(
                        msB_ps[:, :],
                        attnT[:, slot(c) * 512 + 128 * m: slot(c) * 512 + 128 * (m + 1)],
                        wout_sb[:, 1024 * c + 512 * n: 1024 * c + 512 * (n + 1)],
                        start=(c == 0), stop=(c == 7)),
                        "TS" if c == 7 else None)
                if m == 3 and n == 1:
                    for c in range(8):
                        st["slot_free"][slot(c)] = ts_w
                em.wait("vector", "TS", ts_w)
                st["ms"] = em.op("vector", lambda E, m=m, n=n: E.tensor_tensor(
                    Asb[m % 2][:, 512 * n:512 * (n + 1)], msB_ps[:, :],
                    bout_sb[:, 512 * n:512 * (n + 1)], ALU.add), "VS")
            return go

        def g_lnstatA(m):
            def go():
                fv = vfence()
                em.wait("scalar", "VS", fv)
                if g.get("debug"):
                    em.dma("scalar", "PQ",
                           g["s_A"].rearrange("(t p) d -> t p d", p=128)[qb * 4 + m],
                           Asb[m % 2][:, :])
                loc[("sA", m)] = ln_stat(em, Asb[m % 2][:, :], D, lnB[m % 2])
                st["A_free"][m % 2] = loc[("sA", m)]
            return go

        def g_lnfinA(m):
            def go():
                loc[("fA", m)] = ln_fin(em, D, lnB[m % 2], loc[("sA", m)])
            return go

        def g_applyA(m):
            def go():
                vs_ln = ln_apply(em, Asb[m % 2][:, :], lnB[m % 2], loc[("fA", m)],
                                 Asb[m % 2][:, :])
                em.wait("vector", "VS", vs_ln)
                em.wait("vector", "LQ", x_load[(qb, m)])
                em.wait("vector", "PQ", st["y_free"][m % 2])
                em.wait("vector", "SS", st["ysq_free"][m % 2])
                vs_y = em.op("vector", lambda E, m=m: E.tensor_tensor(
                    ysb[m % 2][:, :], Asb[m % 2][:, :], xrt[m % 2][:, :], ALU.add), "VS")
                st["x_free"][m % 2] = vs_y
                em.wait("vector", "VS", vs_y)
                fv = vfence()
                em.wait("scalar", "VS", fv)
                st["y_free"][m % 2] = em.dma("scalar", "PQ", y_tok[qb * 4 + m],
                                             ysb[m % 2][:, :])
                loc[("y", m)] = vs_y
            return go

        def g_lnstatY(m):
            def go():
                em.wait("vector", "VS", loc[("y", m)])
                loc[("sY", m)] = ln_stat(em, ysb[m % 2][:, :], D, lnB[m % 2])
                st["ysq_free"][m % 2] = loc[("sY", m)]
            return go

        def g_lnfinY(m):
            def go():
                loc[("fY", m)] = ln_fin(em, D, lnB[m % 2], loc[("sY", m)])
            return go

        def g_applyY(m):
            def go():
                em.wait("vector", "PQ", st["x2_free"])
                vs_x2 = ln_apply(em, ysb[m % 2][:, :], lnB[m % 2], loc[("fY", m)],
                                 x2b[:, :])
                em.wait("vector", "VS", vs_x2)
                fv = vfence()
                em.wait("scalar", "VS", fv)
                st["x2_free"] = em.dma(
                    "scalar", "PQ",
                    g["s_x2"].rearrange("(t p) d -> t p d", p=128)[qb * 4 + m],
                    x2b[:, :])
                loc["x2sp"] = st["x2_free"]
            return go

        def g_x2t():
            def go():
                em.wait("sync", "PQ", loc["x2sp"])
                em.wait("sync", "TS", st["x2t_free"])
                lq = 0
                for c in range(8):
                    lq = em.tread("sync", "LQ", x2t[:, 512 * c:512 * (c + 1)],
                                  g["s_x2"][QB * qb:QB * (qb + 1), 128 * c:128 * (c + 1)])
                loc["x2t_ld"] = lq
            return go

        def g_t1T():
            def go():
                em.wait("tensor", "LQ", loc["x2t_ld"])
                em.wait("tensor", "VS", st["ms"])
                em.wait("tensor", "GQ", gq_wo)
                ts_1 = None
                for c in range(8):
                    ts_1 = em.op("tensor", lambda E, c=c: E.matmul(
                        msB_ps[0:16, :], vT[:, 16 * c:16 * (c + 1)],
                        x2t[:, 512 * c:512 * (c + 1)],
                        start=(c == 0), stop=(c == 7)), "TS" if c == 7 else None)
                em.wait("vector", "TS", ts_1)
                st["ms"] = em.op("vector", lambda E: E.tensor_copy(
                    t1T[0:16, :], msB_ps[0:16, :]), "VS")
            return go

        def g_fc(n, m):
            def go():
                fsl = n % 2
                em.wait("tensor", "LQ", fcT_load[(qb, n)])
                em.wait("tensor", "LQ", loc["x2t_ld"])
                em.wait("tensor", "VS", st["ms"])
                for c in range(8):
                    em.op("tensor", lambda E, c=c, m=m, fsl=fsl: E.matmul(
                        msB_ps[:, :],
                        x2t[:, 512 * c + 128 * m: 512 * c + 128 * (m + 1)],
                        fcT[fsl][:, 512 * c:512 * (c + 1)],
                        start=(c == 0), stop=False))
                ts_f = em.op("tensor", lambda E, m=m, n=n: E.matmul(
                    msB_ps[:, :], t1T[0:16, 128 * m:128 * (m + 1)],
                    gW2[0:16, 512 * n:512 * (n + 1)], start=False, stop=True), "TS")
                if m == 3:
                    st["fcT_free"][fsl] = ts_f
                    if n == 7:
                        st["x2t_free"] = ts_f
                    nn = n + 2
                    if nn < 8:
                        g_fcload(nn)()
                em.wait("vector", "TS", ts_f)
                sl_r = (n * 4 + m) % 2
                em.wait("vector", "GS", st["rsb_free"][sl_r])
                vs_r = em.op("vector", lambda E, sl_r=sl_r: E.tensor_scalar(
                    rsb[sl_r][:, :], msB_ps[:, :], 0.0, None, ALU.max), "VS")
                st["ms"] = vs_r
                em.wait("gpsimd", "VS", vs_r)
                if m == 0:
                    em.wait("gpsimd", "PQ", st["rr_free"])
                gs_r = em.op("gpsimd", lambda E, sl_r=sl_r, m=m: E.tensor_tensor(
                    rrb[:, 512 * m:512 * (m + 1)], rsb[sl_r][:, :],
                    rsb[sl_r][:, :], ALU.mult), "GS")
                st["rsb_free"][sl_r] = gs_r
                if m == 3:
                    em.wait("scalar", "GS", gs_r)
                    st["rr_free"] = em.dma(
                        "scalar", "PQ",
                        s_rr[qb * 512:(qb + 1) * 512, 512 * n:512 * (n + 1)]
                        .rearrange("(mm p) f -> p mm f", p=128),
                        rrb[:, :].rearrange("p (mm f) -> p mm f", mm=4))
            return go

        seq = [g_fcload(0), g_fcload(1)]
        for m in range(4):
            seq.append(g_xload(m))
            seq.append(g_w(m, 0))
            seq.append(g_w(m, 1))
            seq.append(g_lnstatA(m))
            seq.append(g_lnfinA(m))
            seq.append(g_applyA(m))
            seq.append(g_lnstatY(m))
            seq.append(g_lnfinY(m))
            seq.append(g_applyY(m))
        seq.append(g_x2t())
        seq.append(g_t1T())
        for n in range(8):
            for m in range(4):
                seq.append(g_fc(n, m))
        return seq

    for qb in range(NQB):
        for p_i in range(NP):
            sl = p_i % 2
            qt_prefetch(qb, p_i)
            nq, npair = (qb, p_i + 1) if p_i + 1 < NP else (qb + 1, 0)
            qt_prefetch(nq, npair)
            em.wait("tensor", "LQ", qt_load[(qb, p_i)])
            em.wait("tensor", "LQ", lq_v)
            col = qb * 8 + p_i
            jA = (2 * p_i) % 4
            jB = (2 * p_i + 1) % 4
            for cc in range(32):
                qt_i = cc // 8
                if cc % 8 == 0:
                    for dg in (1, 2):
                        qt_n = qt_i + dg
                        tgt = (qb, p_i, qt_n) if qt_n < 4 else (nq, npair, qt_n - 4)
                        if tgt[0] < NQB:
                            kq_prefetch(*tgt)
                sslot = kq_slot(qb, p_i, qt_i)
                em.wait("tensor", "LQ", kq_load[(qb, p_i, qt_i)])
                em.wait("tensor", "SS", st["last_exp2"][cc % 2])
                soff = 1024 * (cc % 2)
                ko2 = 128 * (cc - 8 * qt_i)
                em.op("tensor", lambda E, ko2=ko2, soff=soff, sslot=sslot, sl=sl: E.matmul(
                    sc_ps[:, soff:soff + 512],
                    ktb[sslot][0:64, ko2:ko2 + 128],
                    qtb[sl][0:64, :], start=True, stop=True))
                ts_sc = em.op("tensor", lambda E, ko2=ko2, soff=soff, sslot=sslot, sl=sl: E.matmul(
                    sc_ps[:, soff + 512:soff + 1024],
                    ktb[sslot][64:128, ko2:ko2 + 128],
                    qtb[sl][64:128, :], start=True, stop=True), "TS")
                if cc % 8 == 7:
                    st["kq_free"][sslot] = ts_sc
                esl = cc % 4
                em.wait("scalar", "TS", ts_sc)
                em.wait("scalar", "TS", st["exp_free"][esl])
                ss_e = em.op("scalar", lambda E, esl=esl, soff=soff, col=col: E.activation(
                    expb[esl][:, :], sc_ps[:, soff:soff + 1024], AF.Exp,
                    bias=negB[:, col:col + 1], scale=1.0), "SS")
                st["last_exp2"][cc % 2] = ss_e

                def mk_pv(cc=cc, esl=esl, ss_e=ss_e, p_i=p_i, jA=jA, jB=jB):
                    def go_pv():
                        em.wait("tensor", "SS", ss_e)
                        if cc == 0:
                            em.wait("tensor", "VS", st["pv_free"])
                            em.wait("tensor", "VS", st["last_rc"])
                        em.op("tensor",
                              lambda E, cc=cc, p_i=p_i, esl=esl: E.matmul(
                                  pvp[p_i % 2][0:64, :],
                                  vpr[:, cc * D + (2 * p_i) * 64: cc * D + (2 * p_i) * 64 + 64],
                                  expb[esl][:, 0:512],
                                  start=(cc == 0), stop=(cc == 31)))
                        em.op("tensor",
                              lambda E, cc=cc, p_i=p_i, esl=esl: E.matmul(
                                  pvp[p_i % 2][64:128, :],
                                  vpr[:, cc * D + (2 * p_i + 1) * 64: cc * D + (2 * p_i + 1) * 64 + 64],
                                  expb[esl][:, 512:1024],
                                  start=(cc == 0), stop=(cc == 31),
                                  tile_position=(0, 64)))
                        em.op("tensor", lambda E, cc=cc, jA=jA, esl=esl: E.matmul(
                            sums_ps[32 * jA:32 * jA + 1, :], onesb[:, :],
                            expb[esl][:, 0:512],
                            start=(cc == 0), stop=(cc == 31),
                            tile_position=(0, 32 * jA)))
                        ts_pv = em.op("tensor", lambda E, cc=cc, jB=jB, esl=esl: E.matmul(
                            sums_ps[32 * jB:32 * jB + 1, :], onesb[:, :],
                            expb[esl][:, 512:1024],
                            start=(cc == 0), stop=(cc == 31),
                            tile_position=(0, 32 * jB)), "TS")
                        st["exp_free"][esl] = ts_pv
                    return go_pv

                defer_q.append(mk_pv())
                while len(defer_q) > 3:
                    defer_q.pop(0)()
                if cc == 6 and epi_q:
                    epi_q.pop(0)()
                if cc % 3 == 1:
                    misc_pop(1)
            st["qt_free"][sl] = em.cnt["TS"]
            def mk_epi_ve(p_i=p_i, jA=jA, jB=jB):
                def go():
                    em.wait("vector", "TS", st["exp_free"][0])
                    em.wait("vector", "TS", st["exp_free"][1])
                    em.wait("vector", "TS", st["exp_free"][2])
                    em.wait("vector", "TS", st["exp_free"][3])
                    em.wait("vector", "TS", st["s_last"])
                    v_r1 = em.op("vector", lambda E, jA=jA: E.reciprocal(
                        rcpb[32 * jA:32 * jA + 1, :], sums_ps[32 * jA:32 * jA + 1, :]), "VS")
                    em.wait("vector", "VS", v_r1)
                    vs_rc = em.op("vector", lambda E, jB=jB: E.reciprocal(
                        rcpb[32 * jB:32 * jB + 1, :], sums_ps[32 * jB:32 * jB + 1, :]), "VS")
                    em.wait("vector", "VS", vs_rc)
                    st["last_rc"] = vfence()
                return go

            def mk_epi_te(p_i=p_i, qb=qb, jA=jA, jB=jB):
                def go():
                    em.wait("tensor", "VS", st["last_rc"])
                    em.wait("tensor", "VS", st["ms"])
                    em.op("tensor", lambda E, jA=jA: E.matmul(
                        msB_ps[0:64, :], ones32[32 * jA:32 * jA + 1, 0:64],
                        rcpb[32 * jA:32 * jA + 1, :], start=True, stop=True,
                        tile_position=(32 * jA, 0)))
                    ts_s = em.op("tensor", lambda E, jB=jB: E.matmul(
                        msB_ps[64:128, :], ones32[32 * jB:32 * jB + 1, 0:64],
                        rcpb[32 * jB:32 * jB + 1, :], start=True, stop=True,
                        tile_position=(32 * jB, 64)), "TS")
                    st["s_last"] = ts_s
                    em.wait("vector", "TS", ts_s)
                    st["ms"] = em.op("vector", lambda E: E.tensor_copy(
                        Ssb[:, :], msB_ps[:, :]), "VS")
                    em.wait("vector", "VS", st["ms"])
                    pslot = (qb * 8 + p_i) % 10
                    em.wait("vector", "TS", st["slot_free"].get(pslot, 0))
                    st["pv_free"] = em.op("vector", lambda E, pslot=pslot, p_i=p_i: E.tensor_tensor(
                        attnT[:, pslot * 512:(pslot + 1) * 512], pvp[p_i % 2][:, :],
                        Ssb[:, :], ALU.mult), "VS")
                    if g.get("debug"):
                        em.wait("scalar", "VS", st["pv_free"])
                        em.dma("scalar", "PQ",
                               g["s_attn"][(qb * 8 + p_i) * 128:(qb * 8 + p_i + 1) * 128, :],
                               attnT[:, pslot * 512:(pslot + 1) * 512])
                return go

            defer_q.append(mk_epi_ve())
            epi_q.append(mk_epi_te())
        misc_q.extend(make_misc(qb))
    while defer_q:
        defer_q.pop(0)()
    while epi_q:
        epi_q.pop(0)()
    misc_pop(len(misc_q))


def emit_D(em, g):
    (pjT, rrT, yD, fD, oD, lnD, pj_ps) = (g[k] for k in [
        "pjT", "rrT", "yD", "fD", "oD", "lnD", "pj_ps"])
    s_pjeff, s_rr, s_y, out = g["s_pjeff"], g["s_rr"], g["s_y"], g["out"]

    lq_pj = 0
    for k in range(32):
        lq_pj = em.tread("sync", "LQ", pjT[:, 1024 * k:1024 * (k + 1)],
                         s_pjeff[0:D, 128 * k:128 * (k + 1)])
    rr_load, y_load = {}, {}
    fsc = g["fscB"]
    last_sq_ss = [0]
    rr_free = {0: 0, 1: 0}
    o_free = {0: 0, 1: 0}
    y_free = {0: 0, 1: 0}
    pj_free = {i: 0 for i in range(4)}

    def load_slab(sl):
        ssl = sl % 2
        em.wait("sync", "TS", rr_free[ssl])
        lq = 0
        for k in range(32):
            lq = em.tread("sync", "LQ", rrT[ssl][:, 512 * k:512 * (k + 1)],
                          s_rr[sl * 512:(sl + 1) * 512, 128 * k:128 * (k + 1)])
        rr_load[sl] = lq

    load_slab(0)
    load_slab(1)
    y_tok = s_y.rearrange("(t p) d -> t p d", p=128)
    o_tok = out.rearrange("(t p) d -> t p d", p=128)

    def y_prefetch(sl, m):
        if sl >= NQB or m >= 4 or (sl, m) in y_load:
            return
        em.wait("sync", "VS", y_free[m % 2])
        y_load[(sl, m)] = em.dma("sync", "LQ", yD[m % 2][:, :], y_tok[sl * 4 + m])

    y_prefetch(0, 0)
    y_prefetch(0, 1)
    for sl in range(NQB):
        ssl = sl % 2
        em.wait("tensor", "LQ", rr_load[sl])
        em.wait("tensor", "LQ", lq_pj)
        for m in range(4):
            last_ts = {}
            for n in range(2):
                pb = (m * 2 + n) % 4
                em.wait("tensor", "VS", pj_free[pb])
                for k in range(32):
                    tok = em.op("tensor", lambda E, m=m, n=n, k=k, pb=pb, ssl=ssl: E.matmul(
                        pj_ps[pb][:, :],
                        rrT[ssl][:, 512 * k + 128 * m: 512 * k + 128 * (m + 1)],
                        pjT[:, 1024 * k + 512 * n: 1024 * k + 512 * (n + 1)],
                        start=(k == 0), stop=(k == 31)),
                        "TS" if k == 31 else None)
                last_ts[n] = tok
            for n in range(2):
                pb = (m * 2 + n) % 4
                em.wait("vector", "TS", last_ts[n])
                em.wait("vector", "SS", last_sq_ss[0])
                pj_free[pb] = em.op("vector", lambda E, n=n, pb=pb: E.tensor_copy(
                    fD[:, 512 * n:512 * (n + 1)], pj_ps[pb][:, :]), "VS")
            em.wait("vector", "VS", pj_free[(m * 2 + 1) % 4])
            fv = em.op("vector", lambda E: E.tensor_copy(
                fsc[0:1, 0:1], fsc[0:1, 1:2]), "VS")
            em.wait("scalar", "VS", fv)
            if g.get("debug"):
                em.dma("scalar", "PQ",
                       g["s_F"].rearrange("(t p) d -> t p d", p=128)[sl * 4 + m],
                       fD[:, :])
            osl = m % 2
            em.wait("vector", "PQ", o_free[osl])
            vs_lnf, ss_lnf = emit_ln(em, fD[:, :], D, lnD, oD[osl][:, :])
            last_sq_ss[0] = ss_lnf
            em.wait("vector", "VS", vs_lnf)
            em.wait("vector", "LQ", y_load[(sl, m)])
            vs_o = em.op("vector", lambda E, osl=osl, m=m: E.tensor_tensor(
                oD[osl][:, :], oD[osl][:, :], yD[m % 2][:, :], ALU.add), "VS")
            y_free[m % 2] = vs_o
            fo = em.op("vector", lambda E: E.tensor_copy(
                fsc[0:1, 0:1], fsc[0:1, 1:2]), "VS")
            em.wait("scalar", "VS", fo)
            o_free[osl] = em.dma("scalar", "PQ", o_tok[sl * 4 + m], oD[osl][:, :])
            if m + 2 < 4:
                y_prefetch(sl, m + 2)
            else:
                y_prefetch(sl + 1, m - 2)
        rr_free[ssl] = em.cnt["TS"]
        if sl + 2 < NQB:
            load_slab(sl + 2)
    em.wait("sync", "PQ", em.cnt["PQ"])
    em.wait("sync", "LQ", em.cnt["LQ"])
    em.wait("sync", "GQ", em.cnt["GQ"])
    em.wait("sync", "GS", em.cnt["GS"])


def build(debug=False, phases=("A", "B", "D")):
    nc = bass.Bass("TRN2", target_bir_lowering=False)
    em = Em(nc)

    tensors = {}
    t = tensors
    t["x_in"] = nc.dram_tensor("x", [L, D], F32, kind="ExternalInput").ap()
    t["wqkv"] = nc.dram_tensor("wqkv", [D, 3 * D], F32, kind="ExternalInput").ap()
    t["bqkv_r"] = nc.dram_tensor("bqkv_r", [128, 3 * D], F32, kind="ExternalInput").ap()
    t["wout_in"] = nc.dram_tensor("wout", [D, D], F32, kind="ExternalInput").ap()
    t["bout_r"] = nc.dram_tensor("bout_r", [128, D], F32, kind="ExternalInput").ap()
    t["wfc_in"] = nc.dram_tensor("wfc", [FF, D], F32, kind="ExternalInput").ap()
    t["wpj_in"] = nc.dram_tensor("wpj", [D, FF], F32, kind="ExternalInput").ap()
    t["uT_in"] = nc.dram_tensor("uT", [R, FF], F32, kind="ExternalInput").ap()
    t["vT_in"] = nc.dram_tensor("vTt", [128, 128], F32, kind="ExternalInput").ap()
    t["fb_in"] = nc.dram_tensor("fb", [R, R], F32, kind="ExternalInput").ap()
    t["gate_r"] = nc.dram_tensor("gate_r", [R, 1], F32, kind="ExternalInput").ap()
    t["embq_in"] = nc.dram_tensor("embq", [128, NT * HD], F32, kind="ExternalInput").ap()
    t["embk_in"] = nc.dram_tensor("embk", [128, NT * HD], F32, kind="ExternalInput").ap()
    t["ident_in"] = nc.dram_tensor("ident", [128, 128], F32, kind="ExternalInput").ap()
    if "D" in phases:
        t["out"] = nc.dram_tensor("out", [L, D], F32, kind="ExternalOutput").ap()

    ik = "ExternalOutput" if debug else "Internal"
    t["s_q"] = nc.dram_tensor("s_q", [L, D], BF16, kind=ik).ap()
    t["s_k"] = nc.dram_tensor("s_k", [L, D], BF16, kind=ik).ap()
    t["s_v"] = nc.dram_tensor("s_v", [L, VW], BF16, kind=ik).ap()
    t["s_fceff"] = nc.dram_tensor("s_fceff", [FF, D], BF16, kind=ik).ap()
    t["s_fcT"] = nc.dram_tensor("s_fcT", [D, FF], BF16, kind=ik).ap()
    t["s_pjeff"] = nc.dram_tensor("s_pjeff", [D, FF], BF16, kind=ik).ap()
    t["s_y"] = nc.dram_tensor("s_y", [L, D], F32, kind=ik).ap()
    t["s_rr"] = nc.dram_tensor("s_rr", [L, FF], BF16, kind=ik).ap()
    t["s_nb"] = nc.dram_tensor("s_nb", [NQB, NP], F32, kind=ik).ap()
    t["s_attn"] = nc.dram_tensor("s_attn", [NQB * NP * 128, 512], BF16, kind=ik).ap() if debug else None
    t["s_A"] = nc.dram_tensor("s_A", [L, D], F32, kind=ik).ap() if debug else None
    t["s_F"] = nc.dram_tensor("s_F", [L, D], F32, kind=ik).ap() if debug else None
    t["s_x1"] = nc.dram_tensor("s_x1", [L, D], BF16, kind=ik).ap()
    t["s_x2"] = nc.dram_tensor("s_x2", [L, D], BF16, kind=ik).ap()
    t["debug"] = debug

    with ExitStack() as es:
        for nm in ["LQ", "PQ", "GQ", "TS", "VS", "SS", "GS"]:
            em.add_sem(nm, es.enter_context(nc.semaphore(nm)))

        P = dict(side="right")
        t["ident"] = es.enter_context(nc.sbuf_tensor("identb", [128, 128], BF16, **P))
        t["fscB"] = es.enter_context(nc.sbuf_tensor("fsc", [1, 4], F32, **P))
        t["ones32"] = es.enter_context(nc.sbuf_tensor("ones32", [128, 128], F32, **P))
        t["negB"] = es.enter_context(nc.sbuf_tensor("negBr", [128, NQB * NP], F32, **P))
        t["bout_sb"] = es.enter_context(nc.sbuf_tensor("bout_sb", [128, D], F32, **P))
        t["gW2"] = es.enter_context(nc.sbuf_tensor("gW2", [16, FF], BF16, **P))
        t["vT"] = es.enter_context(nc.sbuf_tensor("vT", [128, 8 * R], BF16, **P))

        if "A" in phases:
            with ExitStack() as ea:
                a = dict(t)
                a["wq_sb"] = ea.enter_context(nc.sbuf_tensor("wq_sb", [128, 8 * 3 * D], BF16))
                a["embq_sb"] = ea.enter_context(nc.sbuf_tensor("embq_sb", [128, NT * HD], F32))
                a["embk_sb"] = ea.enter_context(nc.sbuf_tensor("embk_sb", [128, NT * HD], F32))
                a["bqkv_sb"] = ea.enter_context(nc.sbuf_tensor("bqkv_sb", [128, 3 * D], F32))
                a["xt"] = [ea.enter_context(nc.sbuf_tensor(f"xt{i}", [128, D], F32)) for i in range(2)]
                a["x1b"] = [ea.enter_context(nc.sbuf_tensor(f"x1b{i}", [128, D], BF16)) for i in range(2)]
                a["x1t"] = [ea.enter_context(nc.sbuf_tensor(f"x1t{i}", [128, 8 * 512], BF16)) for i in range(2)]
                a["qkvb"] = ea.enter_context(nc.sbuf_tensor("qkvb", [128, 3 * D], F32))
                a["rtmp"] = ea.enter_context(nc.sbuf_tensor("rtmp", [128, D], F32))
                a["qo"] = [ea.enter_context(nc.sbuf_tensor(f"qo{i}", [128, D], BF16)) for i in range(2)]
                a["ko"] = [ea.enter_context(nc.sbuf_tensor(f"ko{i}", [128, D], BF16)) for i in range(2)]
                a["vo"] = [ea.enter_context(nc.sbuf_tensor(f"vo{i}", [128, VW], BF16)) for i in range(2)]
                a["lnA"] = ln_scratch(ea, nc, "lnA", D)
                a["hn"] = ea.enter_context(nc.sbuf_tensor("hn", [128, H], F32))
                a["qn"] = ea.enter_context(nc.sbuf_tensor("qn", [128, NQB * H], F32))
                a["kn"] = ea.enter_context(nc.sbuf_tensor("kn", [128, H], F32))
                a["nbtb"] = ea.enter_context(nc.sbuf_tensor("nbtb", [128, H], BF16))
                a["qm2"] = ea.enter_context(nc.sbuf_tensor("qm2", [16, NQB], F32))
                a["km2"] = ea.enter_context(nc.sbuf_tensor("km2", [16, 1], F32))
                a["Bqk"] = ea.enter_context(nc.sbuf_tensor("Bqk", [16, NQB], F32))
                a["Bqkb"] = ea.enter_context(nc.sbuf_tensor("Bqkb", [16, NQB], BF16))
                a["Btq"] = ea.enter_context(nc.sbuf_tensor("Btq", [8, 16], F32))
                a["pBn"] = ea.enter_context(nc.sbuf_tensor("pBn", [8, NP], F32))
                a["nb164"] = ea.enter_context(nc.sbuf_tensor("nb164", [1, NQB * NP], F32))
                a["wp_in"] = [ea.enter_context(nc.sbuf_tensor(f"wp_in{i}", [128, 2048], F32)) for i in range(2)]
                a["wp_t"] = ea.enter_context(nc.sbuf_tensor("wp_t", [128, 2048], F32))
                a["wp_g"] = ea.enter_context(nc.sbuf_tensor("wp_g", [128, 2048], F32))
                a["wp_o"] = [ea.enter_context(nc.sbuf_tensor(f"wp_o{i}", [128, 2048], BF16)) for i in range(2)]
                a["wp_s"] = ea.enter_context(nc.sbuf_tensor("wp_s", [128, 4], F32))
                a["uT_sb"] = ea.enter_context(nc.sbuf_tensor("uT_sb", [16, FF], BF16))
                a["fb_sb"] = ea.enter_context(nc.sbuf_tensor("fb_sb", [16, R], F32))
                a["gate_sb"] = ea.enter_context(nc.sbuf_tensor("gate_sb", [16, 1], F32))
                a["fbg"] = ea.enter_context(nc.sbuf_tensor("fbg", [16, R], F32))
                a["fbgb"] = ea.enter_context(nc.sbuf_tensor("fbgb", [16, R], BF16))
                a["qkv_ps"] = ea.enter_context(nc.psum_tensor("qkv_ps", [128, 3 * D], F32))
                a["tp_ps"] = ea.enter_context(nc.psum_tensor("tp_ps", [128, 1024], BF16))
                a["ms_ps"] = ea.enter_context(nc.psum_tensor("ms_ps", [128, 512], F32))
                em.phase()
                c0 = em.barrier_counts()
                emit_A(em, a)
                em.check_deadlock(c0)
                with nc.Block("phA") as blk:
                    em.replay(blk)
        cnts_A = em.barrier_counts()

        if "B" in phases:
            with ExitStack() as eb:
                b = dict(t)
                b["vpr"] = eb.enter_context(nc.sbuf_tensor("vpr", [128, NT * VW], BF16))
                b["wout_sb"] = eb.enter_context(nc.sbuf_tensor("wout_sb", [128, 8 * D], BF16))
                b["ktb"] = [eb.enter_context(nc.sbuf_tensor(f"ktb{i}", [128, 1024], BF16)) for i in range(3)]
                b["qtb"] = [eb.enter_context(nc.sbuf_tensor(f"qtb{i}", [128, QB], BF16)) for i in range(2)]
                b["expb"] = [eb.enter_context(nc.sbuf_tensor(f"expb{i}", [128, 1024], BF16)) for i in range(4)]
                b["attnT"] = eb.enter_context(nc.sbuf_tensor("attnT", [128, 10 * 512], BF16))
                b["rcpb"] = eb.enter_context(nc.sbuf_tensor("rcpb", [128, 512], F32))
                b["Ssb"] = eb.enter_context(nc.sbuf_tensor("Ssb", [128, 512], F32))
                b["onesb"] = eb.enter_context(nc.sbuf_tensor("onesb", [128, 1], BF16))
                b["fcT"] = [eb.enter_context(nc.sbuf_tensor(f"fcTb{i}", [128, 8 * 512], BF16)) for i in range(2)]
                b["xrt"] = [eb.enter_context(nc.sbuf_tensor(f"xrt{i}", [128, D], F32)) for i in range(2)]
                b["Asb"] = [eb.enter_context(nc.sbuf_tensor(f"Asb{i}", [128, D], F32)) for i in range(2)]
                b["ysb"] = [eb.enter_context(nc.sbuf_tensor(f"ysb{i}", [128, D], F32)) for i in range(2)]
                b["x2b"] = eb.enter_context(nc.sbuf_tensor("x2b", [128, D], BF16))
                b["x2t"] = eb.enter_context(nc.sbuf_tensor("x2t", [128, 8 * QB], BF16))
                b["t1b"] = eb.enter_context(nc.sbuf_tensor("t1b", [128, R], BF16))
                b["t1T"] = eb.enter_context(nc.sbuf_tensor("t1T", [16, QB], BF16))
                b["rsb"] = [eb.enter_context(nc.sbuf_tensor(f"rsb{i}", [128, 512], F32)) for i in range(2)]
                b["rrb"] = eb.enter_context(nc.sbuf_tensor("rrb", [128, 4 * 512], BF16))
                b["lnB"] = [ln_scratch(eb, nc, f"lnB{i}", D) for i in range(2)]
                b["sc_ps"] = eb.enter_context(nc.psum_tensor("sc_ps", [128, 2048], F32))
                b["pvp"] = [eb.enter_context(nc.psum_tensor(f"pvp{i}", [128, 512], F32)) for i in range(2)]
                b["sums_ps"] = eb.enter_context(nc.psum_tensor("sums_ps", [128, 512], F32))
                b["msB_ps"] = eb.enter_context(nc.psum_tensor("msB_ps", [128, 512], F32))
                em.phase()
                c0 = em.barrier_counts()
                em.wait_all(cnts_A)
                emit_B(em, b)
                em.check_deadlock(c0)
                with nc.Block("phB") as blk:
                    em.replay(blk)
        cnts_B = em.barrier_counts()

        if "D" in phases:
            with ExitStack() as ed:
                d = dict(t)
                d["pjT"] = ed.enter_context(nc.sbuf_tensor("pjT", [128, 32 * D], BF16))
                d["rrT"] = [ed.enter_context(nc.sbuf_tensor(f"rrT{i}", [128, 32 * 512], BF16)) for i in range(2)]
                d["yD"] = [ed.enter_context(nc.sbuf_tensor(f"yD{i}", [128, D], F32)) for i in range(2)]
                d["fD"] = ed.enter_context(nc.sbuf_tensor("fD", [128, D], F32))
                d["oD"] = [ed.enter_context(nc.sbuf_tensor(f"oD{i}", [128, D], F32)) for i in range(2)]
                d["lnD"] = ln_scratch(ed, nc, "lnD", D)
                d["pj_ps"] = [ed.enter_context(nc.psum_tensor(f"pj_ps{i}", [128, 512], F32)) for i in range(4)]
                em.phase()
                c0 = em.barrier_counts()
                em.wait_all(cnts_B)
                emit_D(em, d)
                em.check_deadlock(c0)
                with nc.Block("phD") as blk:
                    em.replay(blk)

        em.phase()
        for s in ["LQ", "PQ", "GQ"]:
            em.wait("sync", s, em.cnt[s])
        with nc.Block("join") as blk:
            em.replay(blk)

    return nc


_NC_CACHE = {}


def _get_nc():
    if "nc" not in _NC_CACHE:
        _NC_CACHE["nc"] = build()
    return _NC_CACHE["nc"]


def _emb_tables():
    dim = HD
    theta = 10000.0 ** (-np.arange(0, dim, 2, dtype=np.float64) / dim)
    pos = np.arange(L, dtype=np.float64)
    freqs = pos[:, None] * theta[None, :]
    emb = np.concatenate([np.sin(freqs), np.cos(freqs)], axis=-1).astype(np.float32)
    tiled = emb.reshape(NT, 128, HD).transpose(1, 0, 2).reshape(128, NT * HD)
    return np.ascontiguousarray(tiled)


def make_inputs(L0_attn_x, L0_attn_Wqkv, L0_attn_bqkv, L0_attn_Wout, L0_attn_bout,
                L0_ffn_fast_b, L0_ffn_w_fc, L0_ffn_w_proj, L0_ffn_u, L0_ffn_v,
                L0_ffn_gate):
    x = np.ascontiguousarray(np.asarray(L0_attn_x, dtype=np.float32))
    embk = _emb_tables()
    embq = np.ascontiguousarray(embk / 8.0)
    common = {
        "wqkv": np.ascontiguousarray(np.asarray(L0_attn_Wqkv, np.float32)),
        "bqkv_r": np.ascontiguousarray(np.broadcast_to(
            np.asarray(L0_attn_bqkv, np.float32), (128, 3 * D))),
        "wout": np.ascontiguousarray(np.asarray(L0_attn_Wout, np.float32)),
        "bout_r": np.ascontiguousarray(np.broadcast_to(
            np.asarray(L0_attn_bout, np.float32), (128, D))),
        "wfc": np.ascontiguousarray(np.asarray(L0_ffn_w_fc, np.float32)),
        "wpj": np.ascontiguousarray(np.asarray(L0_ffn_w_proj, np.float32)),
        "uT": np.ascontiguousarray(np.asarray(L0_ffn_u, np.float32).T),
        "vTt": np.ascontiguousarray(
            np.asarray(L0_ffn_v, np.float32).T.reshape(8, 128, R)
            .transpose(1, 0, 2).reshape(128, 128)),
        "fb": np.ascontiguousarray(np.asarray(L0_ffn_fast_b, np.float32)),
        "gate_r": np.ascontiguousarray(np.broadcast_to(
            np.asarray(L0_ffn_gate, np.float32).reshape(1, 1), (R, 1))),
        "embq": embq,
        "embk": embk,
        "ident": np.eye(128, dtype=np.float32),
    }
    return [dict(common, x=np.ascontiguousarray(x[i])) for i in range(8)]


def kernel(L0_attn_x, L0_attn_Wqkv, L0_attn_bqkv, L0_attn_Wout, L0_attn_bout,
           L0_ffn_fast_b, L0_ffn_w_fc, L0_ffn_w_proj, L0_ffn_u, L0_ffn_v,
           L0_ffn_gate):
    in_maps = make_inputs(L0_attn_x, L0_attn_Wqkv, L0_attn_bqkv, L0_attn_Wout,
                          L0_attn_bout, L0_ffn_fast_b, L0_ffn_w_fc, L0_ffn_w_proj,
                          L0_ffn_u, L0_ffn_v, L0_ffn_gate)
    nc = _get_nc()
    res = run_bass_kernel_spmd(nc, in_maps, core_ids=list(range(8)))
    outs = [np.asarray(res.results[i]["out"], dtype=np.float32) for i in range(8)]
    return np.stack(outs, axis=0)
